# version 12
# baseline (speedup 1.0000x reference)
import os
import numpy as np
import ml_dtypes

import concourse.bass as bass
import concourse.mybir as mybir
from concourse.bass_utils import run_bass_kernel_spmd

F32 = mybir.dt.float32
BF16 = mybir.dt.bfloat16
I32 = mybir.dt.int32
AF = mybir.ActivationFunctionType
ALU = mybir.AluOpType
AXL = mybir.AxisListType

D = 4096
HID = 11008
SEQ = 2048
QTOK = 1024
NQH = 32
NKVH = 8
DH = 128
WINDOW = 1024
SM_SCALE = DH ** -0.5
EPS = 1e-5
NEG = -1e9

DT = D // 128
HT = HID // 128
NTT = SEQ // 128
QTT = QTOK // 128


def sig_swap(n):
    i = np.arange(n)
    b6 = (i >> 6) & 1
    b7 = (i >> 7) & 1
    return i - 64 * b6 - 128 * b7 + 128 * b6 + 64 * b7


class Sem:
    def __init__(self, h):
        self.h = h
        self.n = 0

    def inc(self, k=1):
        self.n += k
        return self



def build_program(debug_phase=""):
    nc = bass.Bass("TRN2", target_bir_lowering=False)

    ctx = nc.dram_tensor("ctx", [SEQ, D], F32, kind="ExternalInput").ap()
    wq = nc.dram_tensor("wq", [D, D], F32, kind="ExternalInput").ap()
    wk = nc.dram_tensor("wk", [D, 1024], F32, kind="ExternalInput").ap()
    wv = nc.dram_tensor("wv", [D, 1024], F32, kind="ExternalInput").ap()
    wo = nc.dram_tensor("wo", [D, D], F32, kind="ExternalInput").ap()
    pk1 = nc.dram_tensor("pk1", [32, 64, HID], I32, kind="ExternalInput").ap()
    pk3 = nc.dram_tensor("pk3", [32, 64, HID], I32, kind="ExternalInput").ap()
    pk2 = nc.dram_tensor("pk2", [86, 64, D], I32, kind="ExternalInput").ap()
    s1r = nc.dram_tensor("s1r", [64, HID], F32, kind="ExternalInput").ap()
    s3r = nc.dram_tensor("s3r", [64, HID], F32, kind="ExternalInput").ap()
    s2r = nc.dram_tensor("s2r", [172, D], F32, kind="ExternalInput").ap()
    maskT = nc.dram_tensor("maskT", [SEQ, QTOK], BF16, kind="ExternalInput").ap()
    tabq1 = nc.dram_tensor("tabq1", [128, QTOK], BF16, kind="ExternalInput").ap()
    tabq2 = nc.dram_tensor("tabq2", [128, QTOK], BF16, kind="ExternalInput").ap()
    tabk1 = nc.dram_tensor("tabk1", [128, SEQ], BF16, kind="ExternalInput").ap()
    tabk2 = nc.dram_tensor("tabk2", [128, SEQ], BF16, kind="ExternalInput").ap()
    identw = nc.dram_tensor("identw", [128, 128], BF16, kind="ExternalInput").ap()
    out = nc.dram_tensor("out", [QTOK, D], F32, kind="ExternalOutput").ap()

    qt_h = nc.dram_tensor("qt_h", [NQH, 128, QTOK], BF16, kind="Internal").ap()
    v_h = nc.dram_tensor("v_h", [NTT, 128, 1024], BF16, kind="Internal").ap()
    hact_h = nc.dram_tensor("hact_h", [HT, 128, QTOK], BF16, kind="Internal").ap()
    w2t_h = nc.dram_tensor("w2t_h", [HT, 128, D], BF16, kind="Internal").ap()
    s1b = nc.dram_tensor("s1b", [64, HID], BF16, kind="Internal").ap()
    s3b = nc.dram_tensor("s3b", [64, HID], BF16, kind="Internal").ap()
    s2b = nc.dram_tensor("s2b", [172, D], BF16, kind="Internal").ap()

    dbg = {}
    if debug_phase == "p0":
        dbg["xnt"] = nc.dram_tensor("dbg_xnt", [128, DT * SEQ], BF16,
                                    kind="ExternalOutput").ap()
    if debug_phase == "p123":
        dbg["kt"] = nc.dram_tensor("dbg_kt", [128, NKVH * SEQ], F32,
                                   kind="ExternalOutput").ap()
        dbg["qt"] = nc.dram_tensor("dbg_qt", [NQH * 128, QTOK], F32,
                                   kind="ExternalOutput").ap()
        dbg["v"] = nc.dram_tensor("dbg_v", [NTT * 128, 1024], F32,
                                  kind="ExternalOutput").ap()
        dbg["bq"] = nc.dram_tensor("dbg_bq", [1, NQH * 2], F32,
                                   kind="ExternalOutput").ap()
        dbg["mk"] = nc.dram_tensor("dbg_mk", [1, 16], F32,
                                   kind="ExternalOutput").ap()
    if debug_phase == "p5":
        dbg["s"] = nc.dram_tensor("dbg_s", [QTOK, D], F32,
                                  kind="ExternalOutput").ap()
    if debug_phase == "p4":
        dbg["hact"] = nc.dram_tensor("dbg_hact", [HT * 128, QTOK], F32,
                                     kind="ExternalOutput").ap()
        dbg["w2t"] = nc.dram_tensor("dbg_w2t", [HT * 128, D], F32,
                                    kind="ExternalOutput").ap()

    from contextlib import ExitStack
    with ExitStack() as st:
        en = st.enter_context

        NB16 = 94208
        B16 = en(nc.sbuf_tensor("B16", [128, NB16], BF16))
        B32 = en(nc.sbuf_tensor("B32", [128, 2600], F32))
        BI = en(nc.sbuf_tensor("BI", [128, 3072], I32))
        ident = en(nc.sbuf_tensor("ident", [128, 128], BF16))
        ones_bf = en(nc.sbuf_tensor("ones_bf", [128, 1], BF16))
        ones_f1 = en(nc.sbuf_tensor("ones_f1", [1, 128], F32))
        mkv = en(nc.sbuf_tensor("mkv", [1, 16], F32))
        bqv = en(nc.sbuf_tensor("bqv", [1, 64], F32))

        PS = [en(nc.psum_tensor(f"ps{i}", [128, 512], F32)) for i in range(6)]
        pt_stack = st.enter_context(ExitStack())
        PT = [pt_stack.enter_context(nc.psum_tensor(f"pt{i}", [128, 1024], BF16))
              for i in range(2)]

        sems = [Sem(en(nc.semaphore(f"sm{i}"))) for i in range(100)]
        sem_iter = iter(sems)

        def S():
            return next(sem_iter)

        sBAR = S()

        block = en(nc.Block())

        def b16(off, cols, rows=128, p0=0):
            return B16[p0:p0 + rows, off:off + cols]

        def b32(off, cols, rows=128, p0=0):
            return B32[p0:p0 + rows, off:off + cols]

        def bint(off, cols, rows=128):
            return BI[:rows, off:off + cols]

        XNQ = 0
        XNK = 32768
        KT = 65536
        TR = 81920
        NTR = NB16 - TR

        def barrier():
            makers = [block.sync, block.gpsimd, block.vector, block.scalar,
                      block.tensor]
            target = sBAR.n + len(makers)
            for mk in makers:
                @mk
                def _(eng):
                    eng.drain()
                    eng.sem_inc(sBAR.h, 1)
                    eng.wait_ge(sBAR.h, target)
            sBAR.n = target

        sX = [S(), S()]
        sCN = S()
        sID = S()
        sXN = S()
        sTR0 = S()
        sEV0 = S()

        XRING = TR
        XNSLOT = TR + 8192
        assert NTR >= 12288

        @block.gpsimd
        def _(g):
            g.dma_start(out=ident[:], in_=identw[:, :]).then_inc(sID.inc(16).h, 16)
            g.memset(ones_bf[:], 1.0)
            g.memset(ones_f1[:], 1.0)
            g.memset(mkv[:], 0.0)
            g.memset(bqv[:], 0.0)
            g.dma_start(out=s1b[:, :], in_=s1r[:, :]).then_inc(sCN.inc(16).h, 16)
            g.dma_start(out=s3b[:, :], in_=s3r[:, :]).then_inc(sCN.inc(16).h, 16)
            g.dma_start(out=s2b[:, :], in_=s2r[:, :]).then_inc(sCN.inc(16).h, 16)

        @block.gpsimd
        def _(g):
            for tt in range(NTT):
                if tt >= 2:
                    g.wait_ge(sXN.h, tt - 1)
                for par in range(2):
                    for bb in range(2):
                        src = bass.AP(ctx.tensor,
                                      tt * 128 * D + 64 * par + 128 * bb,
                                      [[D, 128], [256, 16], [1, 64]])
                        dst = bass.AP(B16,
                                      XRING + (tt % 2) * 4096 + 128 * par + 64 * bb,
                                      [[NB16, 128], [256, 16], [1, 64]])
                        g.dma_start(out=dst, in_=src) \
                            .then_inc(sX[tt % 2].inc(16).h, 16)

        SUM, SQA, NM, RS, VAR = 0, 1, 2, 3, 4
        sRS = S()
        sTRP = S()
        sEVP = S()
        sSQ = S()
        sVR = S()

        @block.vector
        def _(v):
            for tt in range(NTT):
                xs = XRING + (tt % 2) * 4096
                xn = XNSLOT
                v.wait_ge(sX[tt % 2].h, 64 * (tt // 2 + 1))
                v.tensor_reduce(out=b32(SUM, 1), in_=b16(xs, 4096),
                                axis=AXL.X, op=ALU.add)
                v.wait_ge(sSQ.h, tt + 1)
                v.drain()
                v.tensor_scalar(out=b32(NM, 1), in0=b32(SUM, 1),
                                scalar1=-1.0 / D, scalar2=None, op0=ALU.mult)
                v.drain()
                v.scalar_tensor_tensor(out=b32(VAR, 1), in0=b32(NM, 1),
                                       scalar=-1.0, in1=b32(NM, 1),
                                       op0=ALU.mult, op1=ALU.mult)
                v.drain()
                v.scalar_tensor_tensor(out=b32(VAR, 1), in0=b32(SQA, 1),
                                       scalar=1.0 / D, in1=b32(VAR, 1),
                                       op0=ALU.mult, op1=ALU.add)
                v.drain()
                v.tensor_scalar(out=b32(VAR, 1), in0=b32(VAR, 1),
                                scalar1=EPS, scalar2=None, op0=ALU.add)
                v.drain()
                v.reciprocal(out=b32(VAR, 1), in_=b32(VAR, 1)) \
                    .then_inc(sVR.inc().h, 1)
                v.wait_ge(sRS.h, tt + 1)
                v.tensor_scalar(out=b16(xn, 2048), in0=b16(xs, 2048),
                                scalar1=b32(NM, 1), scalar2=b32(RS, 1),
                                op0=ALU.add, op1=ALU.mult)
                v.tensor_scalar(out=b16(xn + 2048, 2048), in0=b16(xs + 2048, 2048),
                                scalar1=b32(NM, 1), scalar2=b32(RS, 1),
                                op0=ALU.add, op1=ALU.mult) \
                    .then_inc(sXN.inc().h, 1)

        def emit_evac(sc, tt):
            base = XNK if tt < 8 else XNQ
            col = (tt % 8) * 128
            for grp in range(8):
                gidx = tt * 8 + grp
                sc.wait_ge(sTRP.h, gidx + 1)
                pt = PT[gidx % 2]
                dst = bass.AP(B16, base + (grp * 4) * 1024 + col,
                              [[NB16, 128], [1024, 4], [1, 128]])
                sc.copy(out=dst,
                        in_=pt[:, 0:512].rearrange("p (a b) -> p a b", a=4)) \
                    .then_inc(sEVP.inc().h, 1)

        @block.scalar
        def _(sc):
            for tt in range(NTT):
                if tt >= 1:
                    emit_evac(sc, tt - 1)
                xs = XRING + (tt % 2) * 4096
                sc.wait_ge(sX[tt % 2].h, 64 * (tt // 2 + 1))
                if tt >= 1:
                    sc.wait_ge(sTRP.h, 8 * tt)
                sc.activation(out=b16(XNSLOT, 4096), in_=b16(xs, 4096),
                              func=AF.Square, accum_out=b32(SQA, 1)) \
                    .then_inc(sSQ.inc().h, 1)
                sc.wait_ge(sVR.h, tt + 1)
                sc.activation(out=b32(RS, 1), in_=b32(VAR, 1),
                              func=AF.Sqrt).then_inc(sRS.inc().h, 1)
            emit_evac(sc, NTT - 1)

        @block.tensor
        def _(t):
            gidx = 0
            t.wait_ge(sID.h, 16)
            for tt in range(NTT):
                t.wait_ge(sXN.h, tt + 1)
                xn = XNSLOT
                for grp in range(8):
                    if gidx >= 2:
                        t.wait_ge(sEVP.h, gidx - 1)
                    pt = PT[gidx % 2]
                    for k in range(4):
                        td = grp * 4 + k
                        ins = t.transpose(pt[:, k * 128:(k + 1) * 128],
                                          b16(xn + td * 128, 128), ident[:])
                        if k == 3:
                            ins.then_inc(sTRP.inc().h, 1)
                    gidx += 1

        barrier()

        WCOL = [TR, TR + 4096]
        RB = [TR + 8192, TR + 8704]
        QD = [TR + 9216, TR + 9728]
        VSTG = QD[0]
        TB1 = [TR + 10240, TR + 10752]
        TB2 = [TR + 11264, TR + 11776]
        SCRa, SCRb = 16, 528
        NMT = 1040
        SW32 = [1064, 1576]

        sWC = [S(), S()]
        sTB = [S(), S()]
        sMM = S()
        sKE = S()
        sSW = [S(), S()]
        sRP = S()
        sK2 = S()
        sNM = S()
        sNMc = S()
        sTRV = S()
        sEVV = S()
        sVW = S()
        sQW = [S(), S()]

        segs = [("k", 8, 4, wk, 1024, tabk1, tabk2),
                ("v", 8, 4, wv, 1024, None, None),
                ("q", 32, 2, wq, 4096, tabq1, tabq2)]
        NIT = sum(n_ob * n_tc for _, n_ob, n_tc, _, _, _, _ in segs)

        META = []
        _nr = _nv = _nq = 0
        for kind, n_ob, n_tc, w, wcols, t1, t2 in segs:
            for ob in range(n_ob):
                for tc in range(n_tc):
                    META.append((kind, ob, tc, _nr, _nv, _nq))
                    if kind in ("k", "q"):
                        _nr += 1
                        if kind == "q":
                            _nq += 1
                    else:
                        _nv += 1

        def rhs_off(kind, t, tc):
            if kind == "q":
                return XNQ + t * 1024 + tc * 512
            return (XNK if tc < 2 else XNQ) + t * 1024 + (tc % 2) * 512

        def rb_free_wait(eng, i):
            if i < 2:
                return
            kind, ob, tc, nr, nv, nq = META[i - 2]
            if kind in ("k", "q"):
                eng.wait_ge(sNM.h, nr + 1)
            else:
                eng.wait_ge(sTRV.h, nv + 1)

        OBI = []
        _i = 0
        for kind, n_ob, n_tc, w, wcols, t1, t2 in segs:
            for ob in range(n_ob):
                OBI.append(_i)
                _i += n_tc

        @block.gpsimd
        def _(g):
            ob_global = 0
            wc_cnt = [0, 0]
            i = 0
            for kind, n_ob, n_tc, w, wcols, t1, t2 in segs:
                for ob in range(n_ob):
                    sl = ob_global % 2
                    if ob_global >= 2:
                        g.wait_ge(sMM.h, OBI[ob_global - 1])
                    src = bass.AP(w.tensor, ob * 128,
                                  [[wcols, 128], [128 * wcols, 32], [1, 128]])
                    dst = bass.AP(B16, WCOL[sl],
                                  [[NB16, 128], [128, 32], [1, 128]])
                    g.dma_start(out=dst, in_=src) \
                        .then_inc(sWC[sl].inc(16).h, 16)
                    wc_cnt[sl] += 1
                    for tc in range(n_tc):
                        kindx, obx, tcx, nr, nv, nq = META[i]
                        if kind in ("k", "q"):
                            g.wait_ge(sKE.h, i + 1)
                            if nr >= 2:
                                g.wait_ge(sRP.h, nr - 1)
                            g.dma_start(
                                out=b32(SW32[nr % 2], 512, rows=64, p0=64),
                                in_=b16(RB[i % 2], 512, rows=64, p0=0)) \
                                .then_inc(sSW[nr % 2].inc(16).h, 16)
                            g.dma_start(
                                out=b32(SW32[nr % 2], 512, rows=64, p0=0),
                                in_=b16(RB[i % 2], 512, rows=64, p0=64)) \
                                .then_inc(sSW[nr % 2].inc(16).h, 16)
                        i += 1
                    ob_global += 1

        @block.sync
        def _(s):
            for i in range(NIT):
                kind, ob, tc, nr, nv, nq = META[i]
                if kind in ("k", "q"):
                    sl = nr % 2
                    if nr >= 2:
                        s.wait_ge(sRP.h, nr - 1)
                    s.dma_start(out=b16(TB1[sl], 512),
                                in_=(tabk1 if kind == "k" else tabq1)
                                [:, tc * 512:(tc + 1) * 512]) \
                        .then_inc(sTB[sl].inc(16).h, 16)
                    s.dma_start(out=b16(TB2[sl], 512),
                                in_=(tabk2 if kind == "k" else tabq2)
                                [:, tc * 512:(tc + 1) * 512]) \
                        .then_inc(sTB[sl].inc(16).h, 16)
                if kind == "q":
                    s.wait_ge(sRP.h, nr + 1)
                    s.dma_start(
                        out=bass.AP(qt_h.tensor, ob * 128 * QTOK + tc * 512,
                                    [[QTOK, 128], [1, 512]]),
                        in_=b16(QD[nq % 2], 512)) \
                        .then_inc(sQW[nq % 2].inc(16).h, 16)
                if kind == "v":
                    s.wait_ge(sEVV.h, nv + 1)
                    for j in range(4):
                        m = tc * 4 + j
                        s.dma_start(
                            out=bass.AP(v_h.tensor,
                                        m * 128 * 1024 + ob * 128,
                                        [[1024, 128], [1, 128]]),
                            in_=b16(VSTG + j * 128, 128)) \
                            .then_inc(sVW.inc(16).h, 16)
            s.wait_ge(sQW[0].h, sQW[0].n)
            s.wait_ge(sQW[1].h, sQW[1].n)
            s.wait_ge(sVW.h, sVW.n)

        @block.tensor
        def _(t):
            def deferred(j):
                kind, ob, tc, nr, nv, nq = META[j]
                if kind == "v":
                    t.wait_ge(sKE.h, j + 1)
                    if nv >= 2:
                        t.wait_ge(sEVV.h, nv - 1)
                    pt = PT[nv % 2]
                    for jj in range(4):
                        ins = t.transpose(
                            pt[:, jj * 128:(jj + 1) * 128],
                            b16(RB[j % 2] + jj * 128, 128), ident[:])
                        if jj == 3:
                            ins.then_inc(sTRV.inc().h, 1)
                else:
                    t.wait_ge(sK2.h, nr + 1)
                    if nr >= 1:
                        t.wait_ge(sNMc.h, nr)
                    t.matmul(PS[2][0:1, 0:512], ones_bf[:],
                             b16(RB[j % 2], 512), start=True,
                             stop=True).then_inc(sNM.inc().h, 1)

            ob_global = 0
            wc_seen = [0, 0]
            i = 0
            for kind, n_ob, n_tc, w, wcols, t1, t2 in segs:
                for ob in range(n_ob):
                    sl = ob_global % 2
                    wc_seen[sl] += 1
                    for tc in range(n_tc):
                        ps = PS[i % 2]
                        t.wait_ge(sWC[sl].h, 16 * wc_seen[sl])
                        if i >= 2:
                            t.wait_ge(sKE.h, i - 1)
                        for td in range(DT):
                            ins = t.matmul(ps[:, 0:512],
                                           b16(WCOL[sl] + td * 128, 128),
                                           b16(rhs_off(kind, td, tc), 512),
                                           start=(td == 0), stop=(td == DT - 1))
                            if td == DT - 1:
                                ins.then_inc(sMM.inc().h, 1)
                        if i >= 1:
                            deferred(i - 1)
                        i += 1
                    ob_global += 1
            deferred(NIT - 1)

        @block.scalar
        def _(sc):
            for i in range(NIT):
                kind, ob, tc, nr, nv, nq = META[i]
                sc.wait_ge(sMM.h, i + 1)
                rb_free_wait(sc, i)
                sc.copy(out=b16(RB[i % 2], 512), in_=PS[i % 2][:, 0:512]) \
                    .then_inc(sKE.inc().h, 1)
                if kind == "v":
                    sc.wait_ge(sTRV.h, nv + 1)
                    if nv >= 1:
                        sc.wait_ge(sVW.h, 64 * nv)
                    sc.copy(out=b16(VSTG, 512),
                            in_=PT[nv % 2][:, 0:512]) \
                        .then_inc(sEVV.inc().h, 1)

        KQIT = [i for i in range(NIT) if META[i][0] in ("k", "q")]

        @block.vector
        def _(v):
            def vreduce(i):
                kind, ob, tc, nr, nv, nq = META[i]
                v.wait_ge(sNM.h, nr + 1)
                v.tensor_reduce(out=b32(NMT, 1, rows=1, p0=0),
                                in_=PS[2][0:1, 0:512],
                                axis=AXL.X, op=ALU.max)
                v.drain()
                if kind == "k":
                    if tc == 0:
                        v.tensor_copy(out=mkv[0:1, ob:ob + 1],
                                      in_=b32(NMT, 1, rows=1))
                    else:
                        v.tensor_max(out=mkv[0:1, ob:ob + 1],
                                     in0=mkv[0:1, ob:ob + 1],
                                     in1=b32(NMT, 1, rows=1))
                else:
                    v.tensor_copy(
                        out=bqv[0:1, ob * 2 + tc:ob * 2 + tc + 1],
                        in_=b32(NMT, 1, rows=1))
                v.drain()
                v.sem_inc(sNMc.inc().h, 1)

            for j, i in enumerate(KQIT):
                kind, ob, tc, nr, nv, nq = META[i]
                sl = nr % 2
                v.wait_ge(sSW[sl].h, 32 * (nr // 2 + 1))
                v.wait_ge(sTB[sl].h, 32 * (nr // 2 + 1))
                v.tensor_tensor(out=b32(SCRa, 512),
                                in0=b16(RB[i % 2], 512),
                                in1=b16(TB1[sl], 512), op=ALU.mult)
                v.tensor_tensor(out=b32(SCRb, 512),
                                in0=b32(SW32[sl], 512),
                                in1=b16(TB2[sl], 512), op=ALU.mult)
                v.drain()
                if kind == "k":
                    dest = KT + ob * 2048 + tc * 512
                else:
                    dest = QD[nq % 2]
                    if nq == 0:
                        v.wait_ge(sVW.h, 64 * 32)
                    elif nq >= 2:
                        v.wait_ge(sQW[nq % 2].h, 16 * (nq // 2))
                v.tensor_tensor(out=b16(dest, 512),
                                in0=b32(SCRa, 512),
                                in1=b32(SCRb, 512), op=ALU.add) \
                    .then_inc(sRP.inc().h, 1)
                v.drain()
                v.tensor_tensor(out=b16(RB[i % 2], 512),
                                in0=b16(dest, 512),
                                in1=b16(dest, 512), op=ALU.mult) \
                    .then_inc(sK2.inc().h, 1)
                if j >= 1:
                    vreduce(KQIT[j - 1])
            vreduce(KQIT[-1])

        barrier()

        if debug_phase == "p123":
            dbgsem = S()
            dbgsem2 = S()

            @block.sync
            def _(s):
                s.dma_start(out=dbg["mk"][:, :], in_=mkv[0:1, 0:16]) \
                    .then_inc(dbgsem.inc(16).h, 16)
                s.dma_start(out=dbg["bq"][:, :], in_=bqv[0:1, 0:64]) \
                    .then_inc(dbgsem.inc(16).h, 16)
                s.wait_ge(dbgsem.h, dbgsem.n)

            @block.gpsimd
            def _(g):
                g.dma_start(out=dbg["kt"][:, :], in_=b16(KT, NKVH * SEQ)) \
                    .then_inc(dbgsem2.inc(16).h, 16)
                g.dma_start(out=dbg["qt"][:, :],
                            in_=qt_h[:, :, :].rearrange("a b c -> (a b) c")) \
                    .then_inc(dbgsem2.inc(16).h, 16)
                g.dma_start(out=dbg["v"][:, :],
                            in_=v_h[:, :, :].rearrange("a b c -> (a b) c")) \
                    .then_inc(dbgsem2.inc(16).h, 16)
                g.wait_ge(dbgsem2.h, dbgsem2.n)

        if debug_phase == "p0":
            sDBG = S()

            @block.sync
            def _(s):
                for t in range(DT):
                    s.dma_start(
                        out=bass.AP(dbg["xnt"].tensor, t * SEQ,
                                    [[DT * SEQ, 128], [1, 1024]]),
                        in_=b16(XNK + t * 1024, 1024)).then_inc(sDBG.inc(16).h, 16)
                    s.dma_start(
                        out=bass.AP(dbg["xnt"].tensor, t * SEQ + 1024,
                                    [[DT * SEQ, 128], [1, 1024]]),
                        in_=b16(XNQ + t * 1024, 1024)).then_inc(sDBG.inc(16).h, 16)
                s.wait_ge(sDBG.h, sDBG.n)

        if debug_phase in ("p0", "p123"):
            return nc, dbg

        if debug_phase != "p5":
            DEQ1 = 32768
            DEQ3 = 49152
            SG13 = [81920 + j * 512 for j in range(8)]
            T1B = [86016 + j * 1024 for j in range(4)]
            HSTG = [90112, 91136]
            RW13 = [0, 512]
            TMPH = [1024, 1536]
            TMPL = [2048, 2560]
            TMP_HI, TMP_LO = 2048, 2560

            SUPS = [4] * 21 + [2]
            NSUP = len(SUPS)
            CSUP = [0]
            for t_ in SUPS:
                CSUP.append(CSUP[-1] + t_)

            sRW = [S(), S()]
            sSG = [S() for _ in range(4)]
            sDQ1 = S()
            sDQ3 = S()
            sRDa = S()
            sMP1 = S()
            sMP3 = S()
            sSIL = S()
            sHMr = S()
            sHM = S()
            sHW = [S(), S()]

            def runs_before(g):
                return 2 * sum(2 * t_ for t_ in SUPS[:g])

            def run_info(j):
                g = 0
                while j >= runs_before(g + 1):
                    g += 1
                r = j - runs_before(g)
                S2 = 2 * SUPS[g]
                if r < S2:
                    return True, runs_before(g) // 2 + r
                return False, runs_before(g) // 2 + (r - S2)

            def wait_run_consumed(t, j):
                is_a, idx = run_info(j)
                if is_a:
                    t.wait_ge(sSIL.h, idx + 1)
                else:
                    t.wait_ge(sHMr.h, idx + 1)

            def deq_done_count(n):
                return (n // 32) * 16 + (n % 16) + 1

            def deq_sem_of(n):
                return sDQ1 if (n // 16) % 2 == 0 else sDQ3

            @block.sync
            def _(s):
                nraw = 0

                def emit_raw(s, g, mats=(0, 1)):
                    nonlocal nraw
                    w = 128 * SUPS[g]
                    for mat in mats:
                        pk = pk1 if mat == 0 else pk3
                        for u in range(16):
                            sl = nraw % 2
                            if nraw >= 2:
                                s.wait_ge(sRDa.h, nraw - 1)
                            s.dma_start(
                                out=bint(RW13[sl], w),
                                in_=bass.AP(pk.tensor,
                                            2 * u * 64 * HID + g * 512,
                                            [[64 * HID, 2], [HID, 64],
                                             [1, w]])) \
                                .then_inc(sRW[sl].inc(16).h, 16)
                            nraw += 1

                def emit_hw(s, g):
                    for thl in range(SUPS[g]):
                        idx = CSUP[g] + thl
                        s.wait_ge(sHM.h, idx + 1)
                        s.dma_start(
                            out=bass.AP(hact_h.tensor, idx * 128 * QTOK,
                                        [[QTOK, 128], [1, QTOK]]),
                            in_=b16(HSTG[idx % 2], 1024)) \
                            .then_inc(sHW[idx % 2].inc(16).h, 16)

                emit_raw(s, 0)
                for g in range(1, NSUP):
                    emit_raw(s, g, mats=(0,))
                    emit_hw(s, g - 1)
                    emit_raw(s, g, mats=(1,))
                emit_hw(s, NSUP - 1)

            @block.gpsimd
            def _(gq):
                gq.wait_ge(sCN.h, sCN.n)
                nu = 0
                for g in range(NSUP):
                    w = 128 * SUPS[g]
                    for mat in (0, 1):
                        sb = s1b if mat == 0 else s3b
                        for u in range(16):
                            sl2 = (nu % 4) * 2
                            if nu >= 4:
                                gq.wait_ge(deq_sem_of(nu - 4).h,
                                           deq_done_count(nu - 4))
                            gq.dma_start(
                                out=b16(SG13[sl2], w),
                                in_=bass.AP(sb.tensor, 4 * u * HID + g * 512,
                                            [[2 * HID, 2], [0, 64], [1, w]])) \
                                .then_inc(sSG[nu % 4].inc(16).h, 16)
                            gq.dma_start(
                                out=b16(SG13[sl2 + 1], w),
                                in_=bass.AP(sb.tensor,
                                            (4 * u + 1) * HID + g * 512,
                                            [[2 * HID, 2], [0, 64], [1, w]])) \
                                .then_inc(sSG[nu % 4].inc(16).h, 16)
                            nu += 1

            @block.scalar
            def _(sc):
                for g in range(NSUP):
                    for thl in range(SUPS[g]):
                        for tc in range(2):
                            amat_idx = runs_before(g) // 2 + thl * 2 + tc
                            sc.wait_ge(sMP1.h, amat_idx + 1)
                            if g >= 1:
                                sc.wait_ge(sHM.h,
                                           min(CSUP[g - 1] + thl + 1,
                                               CSUP[g]))
                            sc.activation(
                                out=b16(T1B[thl] + tc * 512, 512),
                                in_=PS[(runs_before(g) + thl * 2 + tc) % 4]
                                [:, 0:512],
                                func=AF.Silu).then_inc(sSIL.inc().h, 1)

            @block.vector
            def _(v):
                def emit_deq(g, mat):
                    dq = sDQ1 if mat == 0 else sDQ3
                    dst = DEQ1 if mat == 0 else DEQ3
                    smp = sMP1 if mat == 0 else sMP3
                    w = 128 * SUPS[g]
                    if g >= 1:
                        v.wait_ge(smp.h, runs_before(g) // 2)
                    for u in range(16):
                        nraw = (g * 2 + mat) * 16 + u
                        v.wait_ge(sRW[nraw % 2].h, 16 * (nraw // 2 + 1))
                        v.wait_ge(sSG[nraw % 4].h, 32 * (nraw // 4 + 1))
                        sl2 = (nraw % 4) * 2
                        tp = nraw % 2
                        v.tensor_scalar(out=bint(TMPH[tp], w),
                                        in0=bint(RW13[nraw % 2], w),
                                        scalar1=4, scalar2=None,
                                        op0=ALU.arith_shift_right)
                        v.tensor_scalar(out=bint(TMPL[tp], w),
                                        in0=bint(RW13[nraw % 2], w),
                                        scalar1=15, scalar2=8,
                                        op0=ALU.bitwise_and,
                                        op1=ALU.bitwise_xor) \
                            .then_inc(sRDa.inc().h, 1)
                        v.drain()
                        v.tensor_tensor(out=b16(dst + (2 * u) * w, w),
                                        in0=bint(TMPH[tp], w),
                                        in1=b16(SG13[sl2], w),
                                        op=ALU.mult)
                        v.scalar_tensor_tensor(
                            out=b16(dst + (2 * u + 1) * w, w),
                            in0=bint(TMPL[tp], w), scalar=-8,
                            in1=b16(SG13[sl2 + 1], w),
                            op0=ALU.add, op1=ALU.mult) \
                            .then_inc(dq.inc().h, 1)

                def emit_hact(g):
                    for thl in range(SUPS[g]):
                        idx = CSUP[g] + thl
                        for tc in range(2):
                            bmat_idx = runs_before(g) // 2 + thl * 2 + tc
                            j = runs_before(g) + 2 * SUPS[g] + thl * 2 + tc
                            v.wait_ge(sMP3.h, bmat_idx + 1)
                            v.wait_ge(sSIL.h, bmat_idx + 1)
                            if idx >= 2:
                                v.wait_ge(sHW[idx % 2].h, 16 * (idx // 2))
                            v.tensor_tensor(
                                out=b16(HSTG[idx % 2] + tc * 512, 512),
                                in0=b16(T1B[thl] + tc * 512, 512),
                                in1=PS[j % 4][:, 0:512], op=ALU.mult) \
                                .then_inc(sHMr.inc().h, 1)
                            if tc == 1:
                                v.drain()
                                v.sem_inc(sHM.inc().h, 1)

                emit_deq(0, 0)
                emit_deq(0, 1)
                for g in range(1, NSUP):
                    emit_deq(g, 0)
                    emit_hact(g - 1)
                    emit_deq(g, 1)
                emit_hact(NSUP - 1)

            @block.tensor
            def _(t):
                for g in range(NSUP):
                    rb = runs_before(g)
                    for mat in (0, 1):
                        dq = sDQ1 if mat == 0 else sDQ3
                        dst = DEQ1 if mat == 0 else DEQ3
                        smp = sMP1 if mat == 0 else sMP3
                        t.wait_ge(dq.h, 16 * (g + 1))
                        w = 128 * SUPS[g]
                        for thl in range(SUPS[g]):
                            for tc in range(2):
                                j = rb + mat * SUPS[g] * 2 + thl * 2 + tc
                                if j >= 4:
                                    wait_run_consumed(t, j - 4)
                                ps = PS[j % 4]
                                for td in range(DT):
                                    ins = t.matmul(
                                        ps[:, 0:512],
                                        b16(dst + td * w + thl * 128, 128),
                                        b16(XNQ + td * 1024 + tc * 512, 512),
                                        start=(td == 0), stop=(td == DT - 1))
                                    if td == DT - 1:
                                        ins.then_inc(smp.inc().h, 1)

            @block.sync
            def _(s):
                s.wait_ge(sHW[0].h, sHW[0].n)
                s.wait_ge(sHW[1].h, sHW[1].n)

            barrier()


            if debug_phase == "p4":
                dbg4 = S()

                @block.gpsimd
                def _(g):
                    g.dma_start(out=dbg["hact"][:, :],
                                in_=hact_h[:, :, :].rearrange(
                                    "a b c -> (a b) c")) \
                        .then_inc(dbg4.inc(16).h, 16)
                    g.dma_start(out=dbg["w2t"][:, :],
                                in_=w2t_h[:, :, :].rearrange(
                                    "a b c -> (a b) c")) \
                        .then_inc(dbg4.inc(16).h, 16)
                    g.wait_ge(dbg4.h, dbg4.n)
                return nc, dbg

        OT = 0
        MSK = 32768
        SS = 32768
        PTB = [49152 + j * 512 for j in range(16)]
        QH = [57344, 58368]
        VH = [59392, 61440]
        RCB = [63488, 64000]
        WOB = 65536
        BIAS = [1540, 1541]
        RECIP = 16
        BB = 1544

        def kts_of(c):
            lo = (c * 512 + 1) // 128
            hi = min(NTT - 1, (c * 512 + 511 + WINDOW + 127) // 128)
            return list(range(lo, hi + 1))

        CHUNKS = [(kvh, qh4, c) for kvh in range(NKVH) for qh4 in range(4)
                  for c in range(2)]
        TILES = []
        for ci, (kvh, qh4, c) in enumerate(CHUNKS):
            for kt in kts_of(c):
                TILES.append((ci, kvh, qh4, c, kt))
        NTIL = len(TILES)
        first_of = {}
        last_of = {}
        for ti, (ci, kvh, qh4, c, kt) in enumerate(TILES):
            first_of.setdefault(ci, ti)
            last_of[ci] = ti

        sMSK = S()
        sVHs = [S(), S()]
        sQHs = [S(), S()]
        sSTm = S()
        sMA = S()
        sEX = S()
        sAV = S()
        sRC = S()
        sBCm = S()
        sRB = S()
        sOE = S()
        sBB = S()
        sBQ = S()
        sBM = S()
        sBE = S()

        @block.sync
        def _(s):
            s.dma_start(out=b16(MSK, 16384),
                        in_=bass.AP(maskT.tensor, 0,
                                    [[QTOK, 128], [QTOK * 128, 16], [1, QTOK]])) \
                .then_inc(sMSK.inc(16).h, 16)
            for kvh in range(NKVH):
                if kvh >= 2:
                    s.wait_ge(sAV.h, last_of[(kvh - 1) * 8 - 1] + 1)
                s.dma_start(out=b16(VH[kvh % 2], 2048),
                            in_=bass.AP(v_h.tensor, kvh * 128,
                                        [[1024, 128], [128 * 1024, 16], [1, 128]])) \
                    .then_inc(sVHs[kvh % 2].inc(16).h, 16)
                for qh4 in range(4):
                    h = kvh * 4 + qh4
                    if h >= 2:
                        s.wait_ge(sSTm.h, last_of[(h - 1) * 2 - 1] + 1)
                    s.dma_start(out=b16(QH[h % 2], 1024),
                                in_=bass.AP(qt_h.tensor, h * 128 * QTOK,
                                            [[QTOK, 128], [1, QTOK]])) \
                        .then_inc(sQHs[h % 2].inc(16).h, 16)


        @block.vector
        def _(v):
            for ci, (kvh, qh4, c) in enumerate(CHUNKS):
                h = kvh * 4 + qh4
                v.tensor_tensor(out=b32(BB, 1, rows=1),
                                in0=bqv[0:1, h * 2 + c:h * 2 + c + 1],
                                in1=mkv[0:1, kvh:kvh + 1], op=ALU.mult) \
                    .then_inc(sBB.inc().h, 1)
                for ti in range(first_of[ci], last_of[ci] + 1):
                    v.wait_ge(sSTm.h, ti + 1)
                    kt = TILES[ti][4]
                    ps = PS[ti % 3]
                    v.tensor_tensor(out=ps[:, 0:512], in0=ps[:, 0:512],
                                    in1=b16(MSK + kt * 1024 + c * 512, 512),
                                    op=ALU.add).then_inc(sMA.inc().h, 1)
                v.wait_ge(sAV.h, last_of[ci] + 1)
                v.reciprocal(out=b32(RECIP, 512, rows=1),
                             in_=PS[5][0:1, 0:512]).then_inc(sRC.inc().h, 1)
                v.wait_ge(sRB.h, ci + 1)
                v.tensor_tensor(out=b16(OT + h * 1024 + c * 512, 512),
                                in0=PS[4][:, 0:512],
                                in1=b16(RCB[ci % 2], 512), op=ALU.mult) \
                    .then_inc(sOE.inc().h, 1)

        @block.scalar
        def _(sc):
            for ci, (kvh, qh4, c) in enumerate(CHUNKS):
                sc.wait_ge(sBB.h, ci + 1)
                sc.activation(out=b32(BB, 1, rows=1), in_=b32(BB, 1, rows=1),
                              func=AF.Sqrt).then_inc(sBQ.inc().h, 1)
                sc.wait_ge(sBM.h, ci + 1)
                sc.activation(out=b32(BIAS[ci % 2], 1), in_=PS[3][:, 0:1],
                              func=AF.Copy, scale=-SM_SCALE) \
                    .then_inc(sBE.inc().h, 1)
                sc.drain()
                for ti in range(first_of[ci], last_of[ci] + 1):
                    sc.wait_ge(sMA.h, ti + 1)
                    if ti >= 16:
                        sc.wait_ge(sAV.h, ti - 15)
                    if ti >= 16:
                        sc.wait_ge(sAV.h, ti - 15)
                    sc.activation(out=b16(PTB[ti % 16], 512),
                                  in_=PS[ti % 3][:, 0:512], func=AF.Exp,
                                  bias=b32(BIAS[ci % 2], 1), scale=SM_SCALE) \
                        .then_inc(sEX.inc().h, 1)
                sc.wait_ge(sBCm.h, ci + 1)
                if ci >= 2:
                    sc.wait_ge(sOE.h, ci - 1)
                sc.copy(out=b16(RCB[ci % 2], 512), in_=PS[3][:, 0:512]) \
                    .then_inc(sRB.inc().h, 1)

        @block.tensor
        def _(t):
            t.wait_ge(sMSK.h, 16)
            for ci, (kvh, qh4, c) in enumerate(CHUNKS):
                h = kvh * 4 + qh4
                t.wait_ge(sBQ.h, ci + 1)
                if ci >= 1:
                    t.wait_ge(sRB.h, ci)
                t.matmul(PS[3][:, 0:1], ones_f1[:], b32(BB, 1, rows=1),
                         start=True, stop=True).then_inc(sBM.inc().h, 1)
                t.wait_ge(sQHs[h % 2].h, 16 * (h // 2 + 1))
                t.wait_ge(sVHs[kvh % 2].h, 16 * (kvh // 2 + 1))
                for ti in range(first_of[ci], last_of[ci] + 1):
                    kt = TILES[ti][4]
                    if ti >= 3:
                        t.wait_ge(sEX.h, ti - 2)
                    t.matmul(PS[ti % 3][:, 0:512],
                             b16(KT + kvh * 2048 + kt * 128, 128),
                             b16(QH[h % 2] + c * 512, 512),
                             start=True, stop=True).then_inc(sSTm.inc().h, 1)
                for ti in range(first_of[ci], last_of[ci] + 1):
                    kt = TILES[ti][4]
                    first = ti == first_of[ci]
                    last = ti == last_of[ci]
                    t.wait_ge(sEX.h, ti + 1)
                    if first:
                        t.wait_ge(sOE.h, ci)
                        t.wait_ge(sRC.h, ci)
                    t.matmul(PS[4][:, 0:512],
                             b16(VH[kvh % 2] + kt * 128, 128),
                             b16(PTB[ti % 16], 512),
                             start=first, stop=last)
                    t.matmul(PS[5][0:1, 0:512], ones_bf[:],
                             b16(PTB[ti % 16], 512),
                             start=first, stop=last).then_inc(sAV.inc().h, 1)
                t.wait_ge(sRC.h, ci + 1)
                t.wait_ge(sBE.h, ci + 1)
                t.matmul(PS[3][:, 0:512], ones_f1[:], b32(RECIP, 512, rows=1),
                         start=True, stop=True).then_inc(sBCm.inc().h, 1)

        barrier()

        sWO = [S(), S()]
        sWMM = S()
        sWEV = S()

        @block.gpsimd
        def _(g):
            for dc_c in range(8):
                for half in range(2):
                    if dc_c >= 1 and half == 0:
                        g.wait_ge(sWMM.h, 8 * dc_c)
                    src = bass.AP(wo.tensor, half * 16 * 128 * D + dc_c * 512,
                                  [[D, 128], [128 * D, 16], [1, 512]])
                    dst = bass.AP(B16, WOB + half * 8192,
                                  [[NB16, 128], [512, 16], [1, 512]])
                    g.dma_start(out=dst, in_=src) \
                        .then_inc(sWO[half].inc(16).h, 16)

        @block.tensor
        def _(t):
            for dc_c in range(8):
                for m in range(8):
                    i = dc_c * 8 + m
                    ps = PS[i % 2]
                    if i >= 2:
                        t.wait_ge(sWEV.h, i - 1)
                    for dct in range(DT):
                        if m == 0 and dct in (0, 16):
                            t.wait_ge(sWO[0 if dct == 0 else 1].h,
                                      16 * (dc_c + 1))
                        ins = t.matmul(ps[:, 0:512],
                                       b16(OT + dct * 1024 + m * 128, 128),
                                       b16(WOB + dct * 512, 512),
                                       start=(dct == 0), stop=(dct == DT - 1))
                        if dct == DT - 1:
                            ins.then_inc(sWMM.inc().h, 1)

        @block.scalar
        def _(sc):
            for dc_c in range(8):
                for m in range(8):
                    i = dc_c * 8 + m
                    sc.wait_ge(sWMM.h, i + 1)
                    sc.copy(out=b16(SS + m * 4096 + dc_c * 512, 512),
                            in_=PS[i % 2][:, 0:512]).then_inc(sWEV.inc().h, 1)

        barrier()

        if debug_phase == "p5":
            dbg5 = S()

            @block.gpsimd
            def _(g):
                for m in range(8):
                    g.dma_start(
                        out=bass.AP(dbg["s"].tensor, m * 128 * D,
                                    [[D, 128], [1, D]]),
                        in_=b16(SS + m * 4096, 4096)) \
                        .then_inc(dbg5.inc(16).h, 16)
                g.wait_ge(dbg5.h, dbg5.n)
            return nc, dbg

        pt_stack.close()
        PS6 = en(nc.psum_tensor("ps6", [128, 512], F32))
        PS7 = en(nc.psum_tensor("ps7", [128, 512], F32))
        PSX = PS + [PS6, PS7]

        HACT6 = [j * 1024 for j in range(4)]
        HRES = 65536
        NRES = 28

        def hact_slot(c8, t2, n):
            if t2 < NRES:
                return HRES + t2 * 1024
            nr = c8 * (HT - NRES) + (t2 - NRES)
            return HACT6[nr % 4]
        W2T6 = [4096 + j * 512 for j in range(20)]
        SG26 = [14336 + j * 512 for j in range(8)]
        XR6 = [16, 528]
        G6 = 1040
        OUTST = [1552, 2064]
        W2RW6 = [0, 512]
        TH6 = [1024, 1536]
        TL6 = [2048, 2560]

        sH6 = [S() for _ in range(4)]
        sHR = S()
        sL26 = [S() for _ in range(4)]
        sDQ6 = S()
        sRD6 = S()
        sM6 = S()
        sSIL6 = S()
        sCB = S()
        sXR = [S(), S()]
        sOW = [S(), S()]

        NP6 = 8 * 43

        @block.sync
        def _(s):
            def emit_loads6(s, c8):
                for t2 in range(HT):
                    n = c8 * HT + t2
                    if t2 < NRES and c8 >= 1:
                        continue
                    if t2 < NRES:
                        s.dma_start(
                            out=b16(HRES + t2 * 1024, 1024),
                            in_=bass.AP(hact_h.tensor, t2 * 128 * QTOK,
                                        [[QTOK, 128], [1, QTOK]])) \
                            .then_inc(sHR.inc(16).h, 16)
                        continue
                    nr = c8 * (HT - NRES) + (t2 - NRES)
                    if nr >= 4:
                        prev_t2 = NRES + (nr - 4) % (HT - NRES)
                        prev_c8 = (nr - 4) // (HT - NRES)
                        s.wait_ge(sM6.h, prev_c8 * HT + prev_t2 + 1)
                    s.dma_start(
                        out=b16(HACT6[nr % 4], 1024),
                        in_=bass.AP(hact_h.tensor, t2 * 128 * QTOK,
                                    [[QTOK, 128], [1, QTOK]])) \
                        .then_inc(sH6[nr % 4].inc(16).h, 16)

            for c8 in range(8):
                emit_loads6(s, c8)

        @block.gpsimd
        def _(g):
            def emit_xr(g, c8x, m):
                idx = c8x * 8 + m
                if idx >= 2:
                    g.wait_ge(sCB.h, idx - 1)
                g.dma_start(
                    out=b32(XR6[idx % 2], 512),
                    in_=bass.AP(ctx.tensor,
                                (1024 + m * 128) * D + c8x * 512,
                                [[D, 128], [1, 512]])) \
                    .then_inc(sXR[idx % 2].inc(16).h, 16)

            for p in range(NP6):
                c8, u2 = p // 43, p % 43
                if c8 >= 1 and u2 < 8:
                    emit_xr(g, c8 - 1, u2)
                sl = p % 4
                sl2 = sl * 2
                if p >= 2:
                    g.wait_ge(sRD6.h, p - 1)
                if p >= 4:
                    g.wait_ge(sDQ6.h, p - 3)
                g.dma_start(
                    out=bint(W2RW6[p % 2], 512),
                    in_=bass.AP(pk2.tensor, 2 * u2 * 64 * D + c8 * 512,
                                [[64 * D, 2], [D, 64], [1, 512]])) \
                    .then_inc(sL26[sl].inc(16).h, 16)
                g.dma_start(
                    out=b16(SG26[sl2], 512),
                    in_=bass.AP(s2b.tensor, 4 * u2 * D + c8 * 512,
                                [[2 * D, 2], [0, 64], [1, 512]])) \
                    .then_inc(sL26[sl].inc(16).h, 16)
                g.dma_start(
                    out=b16(SG26[sl2 + 1], 512),
                    in_=bass.AP(s2b.tensor, (4 * u2 + 1) * D + c8 * 512,
                                [[2 * D, 2], [0, 64], [1, 512]])) \
                    .then_inc(sL26[sl].inc(16).h, 16)
            for m in range(8):
                emit_xr(g, 7, m)

        @block.vector
        def _(v):
            def emit_deq_pair(v, c8, u2):
                p = c8 * 43 + u2
                n = c8 * HT + 2 * u2
                sl = p % 4
                sl2 = sl * 2
                tp = p % 2
                v.wait_ge(sL26[sl].h, 48 * (p // 4 + 1))
                if n + 1 >= 20:
                    v.wait_ge(sM6.h, n + 1 - 20 + 1)
                v.tensor_scalar(out=bint(TH6[tp], 512),
                                in0=bint(W2RW6[p % 2], 512),
                                scalar1=4, scalar2=None,
                                op0=ALU.arith_shift_right)
                v.tensor_scalar(out=bint(TL6[tp], 512),
                                in0=bint(W2RW6[p % 2], 512),
                                scalar1=15, scalar2=8,
                                op0=ALU.bitwise_and, op1=ALU.bitwise_xor) \
                    .then_inc(sRD6.inc().h, 1)
                v.drain()
                v.tensor_tensor(out=b16(W2T6[n % 20], 512),
                                in0=bint(TH6[tp], 512),
                                in1=b16(SG26[sl2], 512), op=ALU.mult)
                v.scalar_tensor_tensor(
                    out=b16(W2T6[(n + 1) % 20], 512),
                    in0=bint(TL6[tp], 512), scalar=-8,
                    in1=b16(SG26[sl2 + 1], 512),
                    op0=ALU.add, op1=ALU.mult).then_inc(sDQ6.inc().h, 1)

            def emit_comb_m(v, c8, m):
                idx = c8 * 8 + m
                v.wait_ge(sSIL6.h, idx + 1)
                v.tensor_tensor(out=b32(G6, 512),
                                in0=b16(SS + m * 4096 + c8 * 512, 512),
                                in1=b32(G6, 512), op=ALU.mult)
                v.drain()
                v.wait_ge(sXR[idx % 2].h, 16 * (idx // 2 + 1))
                if idx >= 2:
                    v.wait_ge(sOW[idx % 2].h, 16 * (idx // 2))
                v.tensor_tensor(out=b32(OUTST[idx % 2], 512),
                                in0=b32(G6, 512),
                                in1=b32(XR6[idx % 2], 512), op=ALU.add) \
                    .then_inc(sCB.inc().h, 1)

            for c8 in range(8):
                for u2 in range(43):
                    emit_deq_pair(v, c8, u2)
                    if c8 >= 1 and 1 <= u2 <= 8:
                        emit_comb_m(v, c8 - 1, u2 - 1)
            for m in range(8):
                emit_comb_m(v, 7, m)

        @block.tensor
        def _(t):
            for c8 in range(8):
                for t2 in range(HT):
                    n = c8 * HT + t2
                    p = c8 * 43 + t2 // 2
                    if t2 < NRES:
                        if c8 == 0 and t2 == 0:
                            t.wait_ge(sHR.h, 16 * NRES)
                    else:
                        nr = c8 * (HT - NRES) + (t2 - NRES)
                        t.wait_ge(sH6[n % 4].h, 16 * (nr // 4 + 1))
                    t.wait_ge(sDQ6.h, p + 1)
                    hs = hact_slot(c8, t2, n)
                    for m in range(8):
                        if c8 >= 1 and t2 == 0:
                            t.wait_ge(sSIL6.h, 8 * (c8 - 1) + m + 1)
                        ins = t.matmul(PSX[m][:, 0:512],
                                       b16(hs + m * 128, 128),
                                       b16(W2T6[n % 20], 512),
                                       start=(t2 == 0), stop=(t2 == HT - 1))
                        if m == 7:
                            ins.then_inc(sM6.inc().h, 1)

        @block.scalar
        def _(sc):
            def emit_out6(sc, idx):
                c8o, mo = idx // 8, idx % 8
                sc.wait_ge(sCB.h, idx + 1)
                sc.dma_start(
                    out=bass.AP(out.tensor, mo * 128 * D + c8o * 512,
                                [[D, 128], [1, 512]]),
                    in_=b32(OUTST[idx % 2], 512)) \
                    .then_inc(sOW[idx % 2].inc(16).h, 16)

            for c8 in range(8):
                for m in range(8):
                    idx = c8 * 8 + m
                    sc.wait_ge(sM6.h, HT * (c8 + 1))
                    if idx >= 1:
                        sc.wait_ge(sCB.h, idx)
                    sc.activation(out=b32(G6, 512), in_=PSX[m][:, 0:512],
                                  func=AF.Silu).then_inc(sSIL6.inc().h, 1)
                    if idx >= 2:
                        emit_out6(sc, idx - 2)
            emit_out6(sc, 62)
            emit_out6(sc, 63)
            sc.wait_ge(sOW[0].h, 16 * 32)
            sc.wait_ge(sOW[1].h, 16 * 32)

        barrier()
        return nc, dbg
    return nc, dbg



def prep_inputs(L0_attn_x, L0_attn_wq, L0_attn_wk, L0_attn_wv, L0_attn_wo,
                L0_ffn_w1, L0_ffn_s1, L0_ffn_w2, L0_ffn_s2, L0_ffn_w3,
                L0_ffn_s3):
    pd = sig_swap(D)
    ph = sig_swap(HID)

    wq_r = np.ascontiguousarray(np.asarray(L0_attn_wq)[pd])
    wk_r = np.ascontiguousarray(np.asarray(L0_attn_wk)[pd])
    wv_r = np.ascontiguousarray(np.asarray(L0_attn_wv)[pd])
    wo_n = np.ascontiguousarray(np.asarray(L0_attn_wo))

    def prep_w13(w):
        w = np.asarray(w).reshape(HID, 32, 64)[ph]
        return np.ascontiguousarray(w.transpose(1, 2, 0))

    def prep_s13(s):
        return np.ascontiguousarray(np.asarray(s).reshape(HID, 64)[ph].T)

    pk1 = prep_w13(L0_ffn_w1)
    pk3 = prep_w13(L0_ffn_w3)
    s1r = prep_s13(L0_ffn_s1)
    s3r = prep_s13(L0_ffn_s3)
    pk2 = np.ascontiguousarray(
        np.asarray(L0_ffn_w2).reshape(D, 86, 64).transpose(1, 2, 0))
    s2r = np.ascontiguousarray(np.asarray(L0_ffn_s2).reshape(D, 172).T)

    x = np.asarray(L0_attn_x)

    inv = 10000.0 ** (-np.arange(64, dtype=np.float64) / 64.0)
    kc = np.arange(SEQ)
    ql = np.arange(QTOK)

    per_half = []
    for half in (0, 1):
        if half == 1:
            qpos = ql + 1024
            kpos = kc.astype(np.float64)
            allowed = (kc[:, None] <= qpos[None, :]) & \
                      (kc[:, None] > qpos[None, :] - WINDOW)
        else:
            qpos = ql
            kpos = np.concatenate([np.arange(1024), np.arange(1024)]).astype(np.float64)
            kp = kc - 1024
            allowed = (kc[:, None] >= 1024) & (kp[:, None] <= ql[None, :])
        def tabs(pos):
            ang = pos[None, :] * inv[:, None]
            c, si = np.cos(ang), np.sin(ang)
            t1 = np.concatenate([c, c], axis=0)
            t2 = np.concatenate([-si, si], axis=0)
            return (t1.astype(ml_dtypes.bfloat16),
                    t2.astype(ml_dtypes.bfloat16))

        tq1, tq2 = tabs(qpos.astype(np.float64))
        tk1, tk2 = tabs(kpos)
        per_half.append(dict(
            maskT=np.where(allowed, 0.0, NEG).astype(ml_dtypes.bfloat16),
            tabq1=tq1, tabq2=tq2, tabk1=tk1, tabk2=tk2,
        ))

    ident = np.eye(128, dtype=ml_dtypes.bfloat16)

    in_maps = []
    for core in range(8):
        b, half = core // 2, core % 2
        ctxc = x[b] if half == 1 else \
            np.concatenate([x[b, 0:1024], x[b, 0:1024]], axis=0)
        in_maps.append(dict(
            ctx=np.ascontiguousarray(ctxc), wq=wq_r, wk=wk_r, wv=wv_r,
            wo=wo_n, pk1=pk1, pk3=pk3, pk2=pk2, s1r=s1r, s3r=s3r, s2r=s2r,
            identw=ident, **per_half[half]))
    return in_maps


def kernel(**inputs):
    in_maps = prep_inputs(**inputs)
    nc, _ = build_program()
    res = run_bass_kernel_spmd(nc, in_maps, core_ids=list(range(8)))
    o = np.zeros((4, SEQ, D), dtype=np.float32)
    for core in range(8):
        b, half = core // 2, core % 2
        o[b, half * 1024:(half + 1) * 1024] = res.results[core]["out"]
    return o


# version 13 (final)
# speedup vs baseline: 1.0962x; 1.0962x over previous
import os
import numpy as np
import ml_dtypes

import concourse.bass as bass
import concourse.mybir as mybir
from concourse.bass_utils import run_bass_kernel_spmd

F32 = mybir.dt.float32
BF16 = mybir.dt.bfloat16
I32 = mybir.dt.int32
AF = mybir.ActivationFunctionType
ALU = mybir.AluOpType
AXL = mybir.AxisListType

D = 4096
HID = 11008
SEQ = 2048
QTOK = 1024
NQH = 32
NKVH = 8
DH = 128
WINDOW = 1024
SM_SCALE = DH ** -0.5
EPS = 1e-5
NEG = -1e9

DT = D // 128
HT = HID // 128
NTT = SEQ // 128
QTT = QTOK // 128


def sig_swap(n):
    i = np.arange(n)
    b6 = (i >> 6) & 1
    b7 = (i >> 7) & 1
    return i - 64 * b6 - 128 * b7 + 128 * b6 + 64 * b7


class Sem:
    def __init__(self, h):
        self.h = h
        self.n = 0

    def inc(self, k=1):
        self.n += k
        return self



def build_program(debug_phase=""):
    nc = bass.Bass("TRN2", target_bir_lowering=False)

    ctx = nc.dram_tensor("ctx", [SEQ, D], F32, kind="ExternalInput").ap()
    wq = nc.dram_tensor("wq", [32 * 128, D], F32, kind="ExternalInput").ap()
    wk = nc.dram_tensor("wk", [8 * 128, D], F32, kind="ExternalInput").ap()
    wv = nc.dram_tensor("wv", [8 * 128, D], F32, kind="ExternalInput").ap()
    wo = nc.dram_tensor("wo", [D, D], F32, kind="ExternalInput").ap()
    pk1 = nc.dram_tensor("pk1", [32, 64, HID], I32, kind="ExternalInput").ap()
    pk3 = nc.dram_tensor("pk3", [32, 64, HID], I32, kind="ExternalInput").ap()
    pk2 = nc.dram_tensor("pk2", [86, 64, D], I32, kind="ExternalInput").ap()
    s1r = nc.dram_tensor("s1r", [64, HID], F32, kind="ExternalInput").ap()
    s3r = nc.dram_tensor("s3r", [64, HID], F32, kind="ExternalInput").ap()
    s2r = nc.dram_tensor("s2r", [172, D], F32, kind="ExternalInput").ap()
    maskT = nc.dram_tensor("maskT", [SEQ, QTOK], BF16, kind="ExternalInput").ap()
    tabq1 = nc.dram_tensor("tabq1", [128, QTOK], BF16, kind="ExternalInput").ap()
    tabq2 = nc.dram_tensor("tabq2", [128, QTOK], BF16, kind="ExternalInput").ap()
    tabk1 = nc.dram_tensor("tabk1", [128, SEQ], BF16, kind="ExternalInput").ap()
    tabk2 = nc.dram_tensor("tabk2", [128, SEQ], BF16, kind="ExternalInput").ap()
    identw = nc.dram_tensor("identw", [128, 128], BF16, kind="ExternalInput").ap()
    out = nc.dram_tensor("out", [QTOK, D], F32, kind="ExternalOutput").ap()

    qt_h = nc.dram_tensor("qt_h", [NQH, 128, QTOK], BF16, kind="Internal").ap()
    v_h = nc.dram_tensor("v_h", [NTT, 128, 1024], BF16, kind="Internal").ap()
    hact_h = nc.dram_tensor("hact_h", [HT, 128, QTOK], BF16, kind="Internal").ap()
    w2t_h = nc.dram_tensor("w2t_h", [HT, 128, D], BF16, kind="Internal").ap()
    s1b = nc.dram_tensor("s1b", [64, HID], BF16, kind="Internal").ap()
    s3b = nc.dram_tensor("s3b", [64, HID], BF16, kind="Internal").ap()
    s2b = nc.dram_tensor("s2b", [172, D], BF16, kind="Internal").ap()

    dbg = {}
    if debug_phase == "p0":
        dbg["xnt"] = nc.dram_tensor("dbg_xnt", [128, DT * SEQ], BF16,
                                    kind="ExternalOutput").ap()
    if debug_phase == "p123":
        dbg["kt"] = nc.dram_tensor("dbg_kt", [128, NKVH * SEQ], F32,
                                   kind="ExternalOutput").ap()
        dbg["qt"] = nc.dram_tensor("dbg_qt", [NQH * 128, QTOK], F32,
                                   kind="ExternalOutput").ap()
        dbg["v"] = nc.dram_tensor("dbg_v", [NTT * 128, 1024], F32,
                                  kind="ExternalOutput").ap()
        dbg["bq"] = nc.dram_tensor("dbg_bq", [1, NQH * 2], F32,
                                   kind="ExternalOutput").ap()
        dbg["mk"] = nc.dram_tensor("dbg_mk", [1, 16], F32,
                                   kind="ExternalOutput").ap()
    if debug_phase == "p5":
        dbg["s"] = nc.dram_tensor("dbg_s", [QTOK, D], F32,
                                  kind="ExternalOutput").ap()
    if debug_phase == "p4":
        dbg["hact"] = nc.dram_tensor("dbg_hact", [HT * 128, QTOK], F32,
                                     kind="ExternalOutput").ap()
        dbg["w2t"] = nc.dram_tensor("dbg_w2t", [HT * 128, D], F32,
                                    kind="ExternalOutput").ap()

    from contextlib import ExitStack
    with ExitStack() as st:
        en = st.enter_context

        NB16 = 94208
        B16 = en(nc.sbuf_tensor("B16", [128, NB16], BF16))
        B32 = en(nc.sbuf_tensor("B32", [128, 2600], F32))
        BI = en(nc.sbuf_tensor("BI", [128, 3072], I32))
        ident = en(nc.sbuf_tensor("ident", [128, 128], BF16))
        ones_bf = en(nc.sbuf_tensor("ones_bf", [128, 1], BF16))
        ones_f1 = en(nc.sbuf_tensor("ones_f1", [1, 128], F32))
        mkv = en(nc.sbuf_tensor("mkv", [1, 16], F32))
        bqv = en(nc.sbuf_tensor("bqv", [1, 64], F32))

        PS = [en(nc.psum_tensor(f"ps{i}", [128, 512], F32)) for i in range(6)]
        pt_stack = st.enter_context(ExitStack())
        PT = [pt_stack.enter_context(nc.psum_tensor(f"pt{i}", [128, 1024], BF16))
              for i in range(2)]

        sems = [Sem(en(nc.semaphore(f"sm{i}"))) for i in range(100)]
        sem_iter = iter(sems)

        def S():
            return next(sem_iter)

        sBAR = S()

        block = en(nc.Block())

        def b16(off, cols, rows=128, p0=0):
            return B16[p0:p0 + rows, off:off + cols]

        def b32(off, cols, rows=128, p0=0):
            return B32[p0:p0 + rows, off:off + cols]

        def bint(off, cols, rows=128):
            return BI[:rows, off:off + cols]

        XNQ = 0
        XNK = 32768
        KT = 65536
        TR = 81920
        NTR = NB16 - TR

        def barrier():
            makers = [block.sync, block.gpsimd, block.vector, block.scalar,
                      block.tensor]
            target = sBAR.n + len(makers)
            for mk in makers:
                @mk
                def _(eng):
                    eng.drain()
                    eng.sem_inc(sBAR.h, 1)
                    eng.wait_ge(sBAR.h, target)
            sBAR.n = target

        sX = [S(), S()]
        sCN = S()
        sID = S()
        sXN = S()
        sTR0 = S()
        sEV0 = S()

        XRING = TR
        XNSLOT = TR + 8192
        assert NTR >= 12288

        @block.gpsimd
        def _(g):
            g.dma_start(out=ident[:], in_=identw[:, :]).then_inc(sID.inc(16).h, 16)
            g.memset(ones_bf[:], 1.0)
            g.memset(ones_f1[:], 1.0)
            g.memset(mkv[:], 0.0)
            g.memset(bqv[:], 0.0)
            g.dma_start(out=s1b[:, :], in_=s1r[:, :]).then_inc(sCN.inc(16).h, 16)
            g.dma_start(out=s3b[:, :], in_=s3r[:, :]).then_inc(sCN.inc(16).h, 16)
            g.dma_start(out=s2b[:, :], in_=s2r[:, :]).then_inc(sCN.inc(16).h, 16)

        @block.gpsimd
        def _(g):
            for tt in range(NTT):
                if tt >= 2:
                    g.wait_ge(sXN.h, tt - 1)
                for par in range(2):
                    for bb in range(2):
                        src = bass.AP(ctx.tensor,
                                      tt * 128 * D + 64 * par + 128 * bb,
                                      [[D, 128], [256, 16], [1, 64]])
                        dst = bass.AP(B16,
                                      XRING + (tt % 2) * 4096 + 128 * par + 64 * bb,
                                      [[NB16, 128], [256, 16], [1, 64]])
                        g.dma_start(out=dst, in_=src) \
                            .then_inc(sX[tt % 2].inc(16).h, 16)

        SUM, SQA, NM, RS, VAR = 0, 1, 2, 3, 4
        sRS = S()
        sTRP = S()
        sEVP = S()
        sSQ = S()
        sVR = S()

        @block.vector
        def _(v):
            for tt in range(NTT):
                xs = XRING + (tt % 2) * 4096
                xn = XNSLOT
                v.wait_ge(sX[tt % 2].h, 64 * (tt // 2 + 1))
                v.tensor_reduce(out=b32(SUM, 1), in_=b16(xs, 4096),
                                axis=AXL.X, op=ALU.add)
                v.wait_ge(sSQ.h, tt + 1)
                v.drain()
                v.tensor_scalar(out=b32(NM, 1), in0=b32(SUM, 1),
                                scalar1=-1.0 / D, scalar2=None, op0=ALU.mult)
                v.drain()
                v.scalar_tensor_tensor(out=b32(VAR, 1), in0=b32(NM, 1),
                                       scalar=-1.0, in1=b32(NM, 1),
                                       op0=ALU.mult, op1=ALU.mult)
                v.drain()
                v.scalar_tensor_tensor(out=b32(VAR, 1), in0=b32(SQA, 1),
                                       scalar=1.0 / D, in1=b32(VAR, 1),
                                       op0=ALU.mult, op1=ALU.add)
                v.drain()
                v.tensor_scalar(out=b32(VAR, 1), in0=b32(VAR, 1),
                                scalar1=EPS, scalar2=None, op0=ALU.add)
                v.drain()
                v.reciprocal(out=b32(VAR, 1), in_=b32(VAR, 1)) \
                    .then_inc(sVR.inc().h, 1)
                v.wait_ge(sRS.h, tt + 1)
                v.tensor_scalar(out=b16(xn, 2048), in0=b16(xs, 2048),
                                scalar1=b32(NM, 1), scalar2=b32(RS, 1),
                                op0=ALU.add, op1=ALU.mult)
                v.tensor_scalar(out=b16(xn + 2048, 2048), in0=b16(xs + 2048, 2048),
                                scalar1=b32(NM, 1), scalar2=b32(RS, 1),
                                op0=ALU.add, op1=ALU.mult) \
                    .then_inc(sXN.inc().h, 1)

        def emit_evac(sc, tt):
            base = XNK if tt < 8 else XNQ
            col = (tt % 8) * 128
            for grp in range(8):
                gidx = tt * 8 + grp
                sc.wait_ge(sTRP.h, gidx + 1)
                pt = PT[gidx % 2]
                dst = bass.AP(B16, base + (grp * 4) * 1024 + col,
                              [[NB16, 128], [1024, 4], [1, 128]])
                sc.copy(out=dst,
                        in_=pt[:, 0:512].rearrange("p (a b) -> p a b", a=4)) \
                    .then_inc(sEVP.inc().h, 1)

        @block.scalar
        def _(sc):
            for tt in range(NTT):
                if tt >= 1:
                    emit_evac(sc, tt - 1)
                xs = XRING + (tt % 2) * 4096
                sc.wait_ge(sX[tt % 2].h, 64 * (tt // 2 + 1))
                if tt >= 1:
                    sc.wait_ge(sTRP.h, 8 * tt)
                sc.activation(out=b16(XNSLOT, 4096), in_=b16(xs, 4096),
                              func=AF.Square, accum_out=b32(SQA, 1)) \
                    .then_inc(sSQ.inc().h, 1)
                sc.wait_ge(sVR.h, tt + 1)
                sc.activation(out=b32(RS, 1), in_=b32(VAR, 1),
                              func=AF.Sqrt).then_inc(sRS.inc().h, 1)
            emit_evac(sc, NTT - 1)

        @block.tensor
        def _(t):
            gidx = 0
            t.wait_ge(sID.h, 16)
            for tt in range(NTT):
                t.wait_ge(sXN.h, tt + 1)
                xn = XNSLOT
                for grp in range(8):
                    if gidx >= 2:
                        t.wait_ge(sEVP.h, gidx - 1)
                    pt = PT[gidx % 2]
                    for k in range(4):
                        td = grp * 4 + k
                        ins = t.transpose(pt[:, k * 128:(k + 1) * 128],
                                          b16(xn + td * 128, 128), ident[:])
                        if k == 3:
                            ins.then_inc(sTRP.inc().h, 1)
                    gidx += 1

        barrier()

        WCOL = [TR, TR + 4096]
        RB = [TR + 8192, TR + 8704]
        QD = [TR + 9216, TR + 9728]
        VSTG = QD[0]
        TB1 = [TR + 10240, TR + 10752]
        TB2 = [TR + 11264, TR + 11776]
        SCRa, SCRb = 16, 528
        NMT = 1040
        SW32 = [1064, 1576]

        sWC = [S(), S()]
        sTB = [S(), S()]
        sMM = S()
        sKE = S()
        sSW = [S(), S()]
        sRP = S()
        sK2 = S()
        sNM = S()
        sNMc = S()
        sTRV = S()
        sEVV = S()
        sVW = S()
        sQW = [S(), S()]

        segs = [("k", 8, 4, wk, 1024, tabk1, tabk2),
                ("v", 8, 4, wv, 1024, None, None),
                ("q", 32, 2, wq, 4096, tabq1, tabq2)]
        NIT = sum(n_ob * n_tc for _, n_ob, n_tc, _, _, _, _ in segs)

        META = []
        _nr = _nv = _nq = 0
        for kind, n_ob, n_tc, w, wcols, t1, t2 in segs:
            for ob in range(n_ob):
                for tc in range(n_tc):
                    META.append((kind, ob, tc, _nr, _nv, _nq))
                    if kind in ("k", "q"):
                        _nr += 1
                        if kind == "q":
                            _nq += 1
                    else:
                        _nv += 1

        def rhs_off(kind, t, tc):
            if kind == "q":
                return XNQ + t * 1024 + tc * 512
            return (XNK if tc < 2 else XNQ) + t * 1024 + (tc % 2) * 512

        def rb_free_wait(eng, i):
            if i < 2:
                return
            kind, ob, tc, nr, nv, nq = META[i - 2]
            if kind in ("k", "q"):
                eng.wait_ge(sNM.h, nr + 1)
            else:
                eng.wait_ge(sTRV.h, nv + 1)

        OBI = []
        _i = 0
        for kind, n_ob, n_tc, w, wcols, t1, t2 in segs:
            for ob in range(n_ob):
                OBI.append(_i)
                _i += n_tc

        @block.gpsimd
        def _(g):
            ob_global = 0
            wc_cnt = [0, 0]
            i = 0
            for kind, n_ob, n_tc, w, wcols, t1, t2 in segs:
                for ob in range(n_ob):
                    sl = ob_global % 2
                    if ob_global >= 2:
                        g.wait_ge(sMM.h, OBI[ob_global - 1])
                    src = bass.AP(w.tensor, ob * 128 * 4096,
                                  [[4096, 128], [128, 32], [1, 128]])
                    dst = bass.AP(B16, WCOL[sl],
                                  [[NB16, 128], [128, 32], [1, 128]])
                    g.dma_start(out=dst, in_=src) \
                        .then_inc(sWC[sl].inc(16).h, 16)
                    wc_cnt[sl] += 1
                    for tc in range(n_tc):
                        kindx, obx, tcx, nr, nv, nq = META[i]
                        if kind in ("k", "q"):
                            g.wait_ge(sKE.h, i + 1)
                            if nr >= 2:
                                g.wait_ge(sRP.h, nr - 1)
                            g.dma_start(
                                out=b32(SW32[nr % 2], 512, rows=64, p0=64),
                                in_=b16(RB[i % 2], 512, rows=64, p0=0)) \
                                .then_inc(sSW[nr % 2].inc(16).h, 16)
                            g.dma_start(
                                out=b32(SW32[nr % 2], 512, rows=64, p0=0),
                                in_=b16(RB[i % 2], 512, rows=64, p0=64)) \
                                .then_inc(sSW[nr % 2].inc(16).h, 16)
                        i += 1
                    ob_global += 1

        @block.sync
        def _(s):
            for i in range(NIT):
                kind, ob, tc, nr, nv, nq = META[i]
                if kind in ("k", "q"):
                    sl = nr % 2
                    if nr >= 2:
                        s.wait_ge(sRP.h, nr - 1)
                    s.dma_start(out=b16(TB1[sl], 512),
                                in_=(tabk1 if kind == "k" else tabq1)
                                [:, tc * 512:(tc + 1) * 512]) \
                        .then_inc(sTB[sl].inc(16).h, 16)
                    s.dma_start(out=b16(TB2[sl], 512),
                                in_=(tabk2 if kind == "k" else tabq2)
                                [:, tc * 512:(tc + 1) * 512]) \
                        .then_inc(sTB[sl].inc(16).h, 16)
                if kind == "q":
                    s.wait_ge(sRP.h, nr + 1)
                    s.dma_start(
                        out=bass.AP(qt_h.tensor, ob * 128 * QTOK + tc * 512,
                                    [[QTOK, 128], [1, 512]]),
                        in_=b16(QD[nq % 2], 512)) \
                        .then_inc(sQW[nq % 2].inc(16).h, 16)
                if kind == "v":
                    s.wait_ge(sEVV.h, nv + 1)
                    for j in range(4):
                        m = tc * 4 + j
                        s.dma_start(
                            out=bass.AP(v_h.tensor,
                                        m * 128 * 1024 + ob * 128,
                                        [[1024, 128], [1, 128]]),
                            in_=b16(VSTG + j * 128, 128)) \
                            .then_inc(sVW.inc(16).h, 16)
            s.wait_ge(sQW[0].h, sQW[0].n)
            s.wait_ge(sQW[1].h, sQW[1].n)
            s.wait_ge(sVW.h, sVW.n)

        @block.tensor
        def _(t):
            def deferred(j):
                kind, ob, tc, nr, nv, nq = META[j]
                if kind == "v":
                    t.wait_ge(sKE.h, j + 1)
                    if nv >= 2:
                        t.wait_ge(sEVV.h, nv - 1)
                    pt = PT[nv % 2]
                    for jj in range(4):
                        ins = t.transpose(
                            pt[:, jj * 128:(jj + 1) * 128],
                            b16(RB[j % 2] + jj * 128, 128), ident[:])
                        if jj == 3:
                            ins.then_inc(sTRV.inc().h, 1)
                else:
                    t.wait_ge(sK2.h, nr + 1)
                    if nr >= 1:
                        t.wait_ge(sNMc.h, nr)
                    t.matmul(PS[2][0:1, 0:512], ones_bf[:],
                             b16(RB[j % 2], 512), start=True,
                             stop=True).then_inc(sNM.inc().h, 1)

            ob_global = 0
            wc_seen = [0, 0]
            i = 0
            for kind, n_ob, n_tc, w, wcols, t1, t2 in segs:
                for ob in range(n_ob):
                    sl = ob_global % 2
                    wc_seen[sl] += 1
                    for tc in range(n_tc):
                        ps = PS[i % 2]
                        t.wait_ge(sWC[sl].h, 16 * wc_seen[sl])
                        if i >= 2:
                            t.wait_ge(sKE.h, i - 1)
                        for td in range(DT):
                            ins = t.matmul(ps[:, 0:512],
                                           b16(WCOL[sl] + td * 128, 128),
                                           b16(rhs_off(kind, td, tc), 512),
                                           start=(td == 0), stop=(td == DT - 1))
                            if td == DT - 1:
                                ins.then_inc(sMM.inc().h, 1)
                        if i >= 1:
                            deferred(i - 1)
                        i += 1
                    ob_global += 1
            deferred(NIT - 1)

        @block.scalar
        def _(sc):
            for i in range(NIT):
                kind, ob, tc, nr, nv, nq = META[i]
                sc.wait_ge(sMM.h, i + 1)
                rb_free_wait(sc, i)
                sc.copy(out=b16(RB[i % 2], 512), in_=PS[i % 2][:, 0:512]) \
                    .then_inc(sKE.inc().h, 1)
                if kind == "v":
                    sc.wait_ge(sTRV.h, nv + 1)
                    if nv >= 1:
                        sc.wait_ge(sVW.h, 64 * nv)
                    sc.copy(out=b16(VSTG, 512),
                            in_=PT[nv % 2][:, 0:512]) \
                        .then_inc(sEVV.inc().h, 1)

        KQIT = [i for i in range(NIT) if META[i][0] in ("k", "q")]

        @block.vector
        def _(v):
            def vreduce(i):
                kind, ob, tc, nr, nv, nq = META[i]
                v.wait_ge(sNM.h, nr + 1)
                v.tensor_reduce(out=b32(NMT, 1, rows=1, p0=0),
                                in_=PS[2][0:1, 0:512],
                                axis=AXL.X, op=ALU.max)
                v.drain()
                if kind == "k":
                    if tc == 0:
                        v.tensor_copy(out=mkv[0:1, ob:ob + 1],
                                      in_=b32(NMT, 1, rows=1))
                    else:
                        v.tensor_max(out=mkv[0:1, ob:ob + 1],
                                     in0=mkv[0:1, ob:ob + 1],
                                     in1=b32(NMT, 1, rows=1))
                else:
                    v.tensor_copy(
                        out=bqv[0:1, ob * 2 + tc:ob * 2 + tc + 1],
                        in_=b32(NMT, 1, rows=1))
                v.drain()
                v.sem_inc(sNMc.inc().h, 1)

            for j, i in enumerate(KQIT):
                kind, ob, tc, nr, nv, nq = META[i]
                sl = nr % 2
                v.wait_ge(sSW[sl].h, 32 * (nr // 2 + 1))
                v.wait_ge(sTB[sl].h, 32 * (nr // 2 + 1))
                v.tensor_tensor(out=b32(SCRa, 512),
                                in0=b16(RB[i % 2], 512),
                                in1=b16(TB1[sl], 512), op=ALU.mult)
                v.tensor_tensor(out=b32(SCRb, 512),
                                in0=b32(SW32[sl], 512),
                                in1=b16(TB2[sl], 512), op=ALU.mult)
                v.drain()
                if kind == "k":
                    dest = KT + ob * 2048 + tc * 512
                else:
                    dest = QD[nq % 2]
                    if nq == 0:
                        v.wait_ge(sVW.h, 64 * 32)
                    elif nq >= 2:
                        v.wait_ge(sQW[nq % 2].h, 16 * (nq // 2))
                v.tensor_tensor(out=b16(dest, 512),
                                in0=b32(SCRa, 512),
                                in1=b32(SCRb, 512), op=ALU.add) \
                    .then_inc(sRP.inc().h, 1)
                v.drain()
                v.tensor_tensor(out=b16(RB[i % 2], 512),
                                in0=b16(dest, 512),
                                in1=b16(dest, 512), op=ALU.mult) \
                    .then_inc(sK2.inc().h, 1)
                if j >= 1:
                    vreduce(KQIT[j - 1])
            vreduce(KQIT[-1])

        barrier()

        if debug_phase == "p123":
            dbgsem = S()
            dbgsem2 = S()

            @block.sync
            def _(s):
                s.dma_start(out=dbg["mk"][:, :], in_=mkv[0:1, 0:16]) \
                    .then_inc(dbgsem.inc(16).h, 16)
                s.dma_start(out=dbg["bq"][:, :], in_=bqv[0:1, 0:64]) \
                    .then_inc(dbgsem.inc(16).h, 16)
                s.wait_ge(dbgsem.h, dbgsem.n)

            @block.gpsimd
            def _(g):
                g.dma_start(out=dbg["kt"][:, :], in_=b16(KT, NKVH * SEQ)) \
                    .then_inc(dbgsem2.inc(16).h, 16)
                g.dma_start(out=dbg["qt"][:, :],
                            in_=qt_h[:, :, :].rearrange("a b c -> (a b) c")) \
                    .then_inc(dbgsem2.inc(16).h, 16)
                g.dma_start(out=dbg["v"][:, :],
                            in_=v_h[:, :, :].rearrange("a b c -> (a b) c")) \
                    .then_inc(dbgsem2.inc(16).h, 16)
                g.wait_ge(dbgsem2.h, dbgsem2.n)

        if debug_phase == "p0":
            sDBG = S()

            @block.sync
            def _(s):
                for t in range(DT):
                    s.dma_start(
                        out=bass.AP(dbg["xnt"].tensor, t * SEQ,
                                    [[DT * SEQ, 128], [1, 1024]]),
                        in_=b16(XNK + t * 1024, 1024)).then_inc(sDBG.inc(16).h, 16)
                    s.dma_start(
                        out=bass.AP(dbg["xnt"].tensor, t * SEQ + 1024,
                                    [[DT * SEQ, 128], [1, 1024]]),
                        in_=b16(XNQ + t * 1024, 1024)).then_inc(sDBG.inc(16).h, 16)
                s.wait_ge(sDBG.h, sDBG.n)

        if debug_phase in ("p0", "p123"):
            return nc, dbg

        if debug_phase != "p5":
            DEQ1 = 32768
            DEQ3 = 49152
            SG13 = [81920 + j * 512 for j in range(8)]
            T1B = [86016 + j * 1024 for j in range(4)]
            HSTG = [90112, 91136]
            RW13 = [0, 512]
            TMPH = [1024, 1536]
            TMPL = [2048, 2560]
            TMP_HI, TMP_LO = 2048, 2560

            SUPS = [4] * 21 + [2]
            NSUP = len(SUPS)
            CSUP = [0]
            for t_ in SUPS:
                CSUP.append(CSUP[-1] + t_)

            sRW = [S(), S()]
            sSG = [S() for _ in range(4)]
            sDQ1 = S()
            sDQ3 = S()
            sRDa = S()
            sMP1 = S()
            sMP3 = S()
            sSIL = S()
            sHMr = S()
            sHM = S()
            sHW = [S(), S()]

            def runs_before(g):
                return 2 * sum(2 * t_ for t_ in SUPS[:g])

            def run_info(j):
                g = 0
                while j >= runs_before(g + 1):
                    g += 1
                r = j - runs_before(g)
                S2 = 2 * SUPS[g]
                if r < S2:
                    return True, runs_before(g) // 2 + r
                return False, runs_before(g) // 2 + (r - S2)

            def wait_run_consumed(t, j):
                is_a, idx = run_info(j)
                if is_a:
                    t.wait_ge(sSIL.h, idx + 1)
                else:
                    t.wait_ge(sHMr.h, idx + 1)

            def deq_done_count(n):
                return (n // 32) * 16 + (n % 16) + 1

            def deq_sem_of(n):
                return sDQ1 if (n // 16) % 2 == 0 else sDQ3

            @block.sync
            def _(s):
                nraw = 0

                def emit_raw(s, g, mats=(0, 1)):
                    nonlocal nraw
                    w = 128 * SUPS[g]
                    for mat in mats:
                        pk = pk1 if mat == 0 else pk3
                        for u in range(16):
                            sl = nraw % 2
                            if nraw >= 2:
                                s.wait_ge(sRDa.h, nraw - 1)
                            s.dma_start(
                                out=bint(RW13[sl], w),
                                in_=bass.AP(pk.tensor,
                                            2 * u * 64 * HID + g * 512,
                                            [[64 * HID, 2], [HID, 64],
                                             [1, w]])) \
                                .then_inc(sRW[sl].inc(16).h, 16)
                            nraw += 1

                def emit_hw(s, g):
                    for thl in range(SUPS[g]):
                        idx = CSUP[g] + thl
                        s.wait_ge(sHM.h, idx + 1)
                        s.dma_start(
                            out=bass.AP(hact_h.tensor, idx * 128 * QTOK,
                                        [[QTOK, 128], [1, QTOK]]),
                            in_=b16(HSTG[idx % 2], 1024)) \
                            .then_inc(sHW[idx % 2].inc(16).h, 16)

                emit_raw(s, 0)
                for g in range(1, NSUP):
                    emit_raw(s, g, mats=(0,))
                    emit_hw(s, g - 1)
                    emit_raw(s, g, mats=(1,))
                emit_hw(s, NSUP - 1)

            @block.gpsimd
            def _(gq):
                gq.wait_ge(sCN.h, sCN.n)
                nu = 0
                for g in range(NSUP):
                    w = 128 * SUPS[g]
                    for mat in (0, 1):
                        sb = s1b if mat == 0 else s3b
                        for u in range(16):
                            sl2 = (nu % 4) * 2
                            if nu >= 4:
                                gq.wait_ge(deq_sem_of(nu - 4).h,
                                           deq_done_count(nu - 4))
                            gq.dma_start(
                                out=b16(SG13[sl2], w),
                                in_=bass.AP(sb.tensor, 4 * u * HID + g * 512,
                                            [[2 * HID, 2], [0, 64], [1, w]])) \
                                .then_inc(sSG[nu % 4].inc(16).h, 16)
                            gq.dma_start(
                                out=b16(SG13[sl2 + 1], w),
                                in_=bass.AP(sb.tensor,
                                            (4 * u + 1) * HID + g * 512,
                                            [[2 * HID, 2], [0, 64], [1, w]])) \
                                .then_inc(sSG[nu % 4].inc(16).h, 16)
                            nu += 1

            @block.scalar
            def _(sc):
                for g in range(NSUP):
                    for thl in range(SUPS[g]):
                        for tc in range(2):
                            amat_idx = runs_before(g) // 2 + thl * 2 + tc
                            sc.wait_ge(sMP1.h, amat_idx + 1)
                            if g >= 1:
                                sc.wait_ge(sHM.h,
                                           min(CSUP[g - 1] + thl + 1,
                                               CSUP[g]))
                            sc.activation(
                                out=b16(T1B[thl] + tc * 512, 512),
                                in_=PS[(runs_before(g) + thl * 2 + tc) % 4]
                                [:, 0:512],
                                func=AF.Silu).then_inc(sSIL.inc().h, 1)

            @block.vector
            def _(v):
                def emit_deq(g, mat):
                    dq = sDQ1 if mat == 0 else sDQ3
                    dst = DEQ1 if mat == 0 else DEQ3
                    smp = sMP1 if mat == 0 else sMP3
                    w = 128 * SUPS[g]
                    if g >= 1:
                        v.wait_ge(smp.h, runs_before(g) // 2)
                    for u in range(16):
                        nraw = (g * 2 + mat) * 16 + u
                        v.wait_ge(sRW[nraw % 2].h, 16 * (nraw // 2 + 1))
                        v.wait_ge(sSG[nraw % 4].h, 32 * (nraw // 4 + 1))
                        sl2 = (nraw % 4) * 2
                        tp = nraw % 2
                        v.tensor_scalar(out=bint(TMPH[tp], w),
                                        in0=bint(RW13[nraw % 2], w),
                                        scalar1=4, scalar2=None,
                                        op0=ALU.arith_shift_right)
                        v.tensor_scalar(out=bint(TMPL[tp], w),
                                        in0=bint(RW13[nraw % 2], w),
                                        scalar1=15, scalar2=8,
                                        op0=ALU.bitwise_and,
                                        op1=ALU.bitwise_xor) \
                            .then_inc(sRDa.inc().h, 1)
                        v.drain()
                        v.tensor_tensor(out=b16(dst + (2 * u) * w, w),
                                        in0=bint(TMPH[tp], w),
                                        in1=b16(SG13[sl2], w),
                                        op=ALU.mult)
                        v.scalar_tensor_tensor(
                            out=b16(dst + (2 * u + 1) * w, w),
                            in0=bint(TMPL[tp], w), scalar=-8,
                            in1=b16(SG13[sl2 + 1], w),
                            op0=ALU.add, op1=ALU.mult) \
                            .then_inc(dq.inc().h, 1)

                def emit_hact(g):
                    for thl in range(SUPS[g]):
                        idx = CSUP[g] + thl
                        for tc in range(2):
                            bmat_idx = runs_before(g) // 2 + thl * 2 + tc
                            j = runs_before(g) + 2 * SUPS[g] + thl * 2 + tc
                            v.wait_ge(sMP3.h, bmat_idx + 1)
                            v.wait_ge(sSIL.h, bmat_idx + 1)
                            if idx >= 2:
                                v.wait_ge(sHW[idx % 2].h, 16 * (idx // 2))
                            v.tensor_tensor(
                                out=b16(HSTG[idx % 2] + tc * 512, 512),
                                in0=b16(T1B[thl] + tc * 512, 512),
                                in1=PS[j % 4][:, 0:512], op=ALU.mult) \
                                .then_inc(sHMr.inc().h, 1)
                            if tc == 1:
                                v.drain()
                                v.sem_inc(sHM.inc().h, 1)

                emit_deq(0, 0)
                emit_deq(0, 1)
                for g in range(1, NSUP):
                    emit_deq(g, 0)
                    emit_hact(g - 1)
                    emit_deq(g, 1)
                emit_hact(NSUP - 1)

            @block.tensor
            def _(t):
                for g in range(NSUP):
                    rb = runs_before(g)
                    for mat in (0, 1):
                        dq = sDQ1 if mat == 0 else sDQ3
                        dst = DEQ1 if mat == 0 else DEQ3
                        smp = sMP1 if mat == 0 else sMP3
                        t.wait_ge(dq.h, 16 * (g + 1))
                        w = 128 * SUPS[g]
                        for thl in range(SUPS[g]):
                            for tc in range(2):
                                j = rb + mat * SUPS[g] * 2 + thl * 2 + tc
                                if j >= 4:
                                    wait_run_consumed(t, j - 4)
                                ps = PS[j % 4]
                                for td in range(DT):
                                    ins = t.matmul(
                                        ps[:, 0:512],
                                        b16(dst + td * w + thl * 128, 128),
                                        b16(XNQ + td * 1024 + tc * 512, 512),
                                        start=(td == 0), stop=(td == DT - 1))
                                    if td == DT - 1:
                                        ins.then_inc(smp.inc().h, 1)

            @block.sync
            def _(s):
                s.wait_ge(sHW[0].h, sHW[0].n)
                s.wait_ge(sHW[1].h, sHW[1].n)

            barrier()


            if debug_phase == "p4":
                dbg4 = S()

                @block.gpsimd
                def _(g):
                    g.dma_start(out=dbg["hact"][:, :],
                                in_=hact_h[:, :, :].rearrange(
                                    "a b c -> (a b) c")) \
                        .then_inc(dbg4.inc(16).h, 16)
                    g.dma_start(out=dbg["w2t"][:, :],
                                in_=w2t_h[:, :, :].rearrange(
                                    "a b c -> (a b) c")) \
                        .then_inc(dbg4.inc(16).h, 16)
                    g.wait_ge(dbg4.h, dbg4.n)
                return nc, dbg

        OT = 0
        MSK = 32768
        SS = 32768
        PTB = [49152 + j * 512 for j in range(16)]
        QH = [57344, 58368]
        VH = [59392, 61440]
        RCB = [63488, 64000]
        WOB = 65536
        BIAS = [1540, 1541]
        RECIP = 16
        BB = 1544

        def kts_of(c):
            lo = (c * 512 + 1) // 128
            hi = min(NTT - 1, (c * 512 + 511 + WINDOW + 127) // 128)
            return list(range(lo, hi + 1))

        CHUNKS = [(kvh, qh4, c) for kvh in range(NKVH) for qh4 in range(4)
                  for c in range(2)]
        TILES = []
        for ci, (kvh, qh4, c) in enumerate(CHUNKS):
            for kt in kts_of(c):
                TILES.append((ci, kvh, qh4, c, kt))
        NTIL = len(TILES)
        first_of = {}
        last_of = {}
        for ti, (ci, kvh, qh4, c, kt) in enumerate(TILES):
            first_of.setdefault(ci, ti)
            last_of[ci] = ti

        sMSK = S()
        sVHs = [S(), S()]
        sQHs = [S(), S()]
        sSTm = S()
        sMA = S()
        sEX = S()
        sAV = S()
        sRC = S()
        sBCm = S()
        sRB = S()
        sOE = S()
        sBB = S()
        sBQ = S()
        sBM = S()
        sBE = S()

        @block.sync
        def _(s):
            s.dma_start(out=b16(MSK, 16384),
                        in_=bass.AP(maskT.tensor, 0,
                                    [[QTOK, 128], [QTOK * 128, 16], [1, QTOK]])) \
                .then_inc(sMSK.inc(16).h, 16)
            for kvh in range(NKVH):
                if kvh >= 2:
                    s.wait_ge(sAV.h, last_of[(kvh - 1) * 8 - 1] + 1)
                s.dma_start(out=b16(VH[kvh % 2], 2048),
                            in_=bass.AP(v_h.tensor, kvh * 128,
                                        [[1024, 128], [128 * 1024, 16], [1, 128]])) \
                    .then_inc(sVHs[kvh % 2].inc(16).h, 16)
                for qh4 in range(4):
                    h = kvh * 4 + qh4
                    if h >= 2:
                        s.wait_ge(sSTm.h, last_of[(h - 1) * 2 - 1] + 1)
                    s.dma_start(out=b16(QH[h % 2], 1024),
                                in_=bass.AP(qt_h.tensor, h * 128 * QTOK,
                                            [[QTOK, 128], [1, QTOK]])) \
                        .then_inc(sQHs[h % 2].inc(16).h, 16)


        @block.vector
        def _(v):
            for ci, (kvh, qh4, c) in enumerate(CHUNKS):
                h = kvh * 4 + qh4
                v.tensor_tensor(out=b32(BB, 1, rows=1),
                                in0=bqv[0:1, h * 2 + c:h * 2 + c + 1],
                                in1=mkv[0:1, kvh:kvh + 1], op=ALU.mult) \
                    .then_inc(sBB.inc().h, 1)
                for ti in range(first_of[ci], last_of[ci] + 1):
                    v.wait_ge(sSTm.h, ti + 1)
                    kt = TILES[ti][4]
                    ps = PS[ti % 3]
                    v.tensor_tensor(out=ps[:, 0:512], in0=ps[:, 0:512],
                                    in1=b16(MSK + kt * 1024 + c * 512, 512),
                                    op=ALU.add).then_inc(sMA.inc().h, 1)
                v.wait_ge(sAV.h, last_of[ci] + 1)
                v.reciprocal(out=b32(RECIP, 512, rows=1),
                             in_=PS[5][0:1, 0:512]).then_inc(sRC.inc().h, 1)
                v.wait_ge(sRB.h, ci + 1)
                v.tensor_tensor(out=b16(OT + h * 1024 + c * 512, 512),
                                in0=PS[4][:, 0:512],
                                in1=b16(RCB[ci % 2], 512), op=ALU.mult) \
                    .then_inc(sOE.inc().h, 1)

        @block.scalar
        def _(sc):
            for ci, (kvh, qh4, c) in enumerate(CHUNKS):
                sc.wait_ge(sBB.h, ci + 1)
                sc.activation(out=b32(BB, 1, rows=1), in_=b32(BB, 1, rows=1),
                              func=AF.Sqrt).then_inc(sBQ.inc().h, 1)
                sc.wait_ge(sBM.h, ci + 1)
                sc.activation(out=b32(BIAS[ci % 2], 1), in_=PS[3][:, 0:1],
                              func=AF.Copy, scale=-SM_SCALE) \
                    .then_inc(sBE.inc().h, 1)
                sc.drain()
                for ti in range(first_of[ci], last_of[ci] + 1):
                    sc.wait_ge(sMA.h, ti + 1)
                    if ti >= 16:
                        sc.wait_ge(sAV.h, ti - 15)
                    if ti >= 16:
                        sc.wait_ge(sAV.h, ti - 15)
                    sc.activation(out=b16(PTB[ti % 16], 512),
                                  in_=PS[ti % 3][:, 0:512], func=AF.Exp,
                                  bias=b32(BIAS[ci % 2], 1), scale=SM_SCALE) \
                        .then_inc(sEX.inc().h, 1)
                sc.wait_ge(sBCm.h, ci + 1)
                if ci >= 2:
                    sc.wait_ge(sOE.h, ci - 1)
                sc.copy(out=b16(RCB[ci % 2], 512), in_=PS[3][:, 0:512]) \
                    .then_inc(sRB.inc().h, 1)

        @block.tensor
        def _(t):
            t.wait_ge(sMSK.h, 16)
            for ci, (kvh, qh4, c) in enumerate(CHUNKS):
                h = kvh * 4 + qh4
                t.wait_ge(sBQ.h, ci + 1)
                if ci >= 1:
                    t.wait_ge(sRB.h, ci)
                t.matmul(PS[3][:, 0:1], ones_f1[:], b32(BB, 1, rows=1),
                         start=True, stop=True).then_inc(sBM.inc().h, 1)
                t.wait_ge(sQHs[h % 2].h, 16 * (h // 2 + 1))
                t.wait_ge(sVHs[kvh % 2].h, 16 * (kvh // 2 + 1))
                for ti in range(first_of[ci], last_of[ci] + 1):
                    kt = TILES[ti][4]
                    if ti >= 3:
                        t.wait_ge(sEX.h, ti - 2)
                    t.matmul(PS[ti % 3][:, 0:512],
                             b16(KT + kvh * 2048 + kt * 128, 128),
                             b16(QH[h % 2] + c * 512, 512),
                             start=True, stop=True).then_inc(sSTm.inc().h, 1)
                for ti in range(first_of[ci], last_of[ci] + 1):
                    kt = TILES[ti][4]
                    first = ti == first_of[ci]
                    last = ti == last_of[ci]
                    t.wait_ge(sEX.h, ti + 1)
                    if first:
                        t.wait_ge(sOE.h, ci)
                        t.wait_ge(sRC.h, ci)
                    t.matmul(PS[4][:, 0:512],
                             b16(VH[kvh % 2] + kt * 128, 128),
                             b16(PTB[ti % 16], 512),
                             start=first, stop=last)
                    t.matmul(PS[5][0:1, 0:512], ones_bf[:],
                             b16(PTB[ti % 16], 512),
                             start=first, stop=last).then_inc(sAV.inc().h, 1)
                t.wait_ge(sRC.h, ci + 1)
                t.wait_ge(sBE.h, ci + 1)
                t.matmul(PS[3][:, 0:512], ones_f1[:], b32(RECIP, 512, rows=1),
                         start=True, stop=True).then_inc(sBCm.inc().h, 1)

        barrier()

        sWO = [S(), S()]
        sWMM = S()
        sWEV = S()

        @block.gpsimd
        def _(g):
            for dc_c in range(8):
                for half in range(2):
                    if dc_c >= 1 and half == 0:
                        g.wait_ge(sWMM.h, 8 * dc_c)
                    src = bass.AP(wo.tensor, half * 16 * 128 * D + dc_c * 512,
                                  [[D, 128], [128 * D, 16], [1, 512]])
                    dst = bass.AP(B16, WOB + half * 8192,
                                  [[NB16, 128], [512, 16], [1, 512]])
                    g.dma_start(out=dst, in_=src) \
                        .then_inc(sWO[half].inc(16).h, 16)

        @block.tensor
        def _(t):
            for dc_c in range(8):
                for m in range(8):
                    i = dc_c * 8 + m
                    ps = PS[i % 2]
                    if i >= 2:
                        t.wait_ge(sWEV.h, i - 1)
                    for dct in range(DT):
                        if m == 0 and dct in (0, 16):
                            t.wait_ge(sWO[0 if dct == 0 else 1].h,
                                      16 * (dc_c + 1))
                        ins = t.matmul(ps[:, 0:512],
                                       b16(OT + dct * 1024 + m * 128, 128),
                                       b16(WOB + dct * 512, 512),
                                       start=(dct == 0), stop=(dct == DT - 1))
                        if dct == DT - 1:
                            ins.then_inc(sWMM.inc().h, 1)

        @block.scalar
        def _(sc):
            for dc_c in range(8):
                for m in range(8):
                    i = dc_c * 8 + m
                    sc.wait_ge(sWMM.h, i + 1)
                    sc.copy(out=b16(SS + m * 4096 + dc_c * 512, 512),
                            in_=PS[i % 2][:, 0:512]).then_inc(sWEV.inc().h, 1)

        barrier()

        if debug_phase == "p5":
            dbg5 = S()

            @block.gpsimd
            def _(g):
                for m in range(8):
                    g.dma_start(
                        out=bass.AP(dbg["s"].tensor, m * 128 * D,
                                    [[D, 128], [1, D]]),
                        in_=b16(SS + m * 4096, 4096)) \
                        .then_inc(dbg5.inc(16).h, 16)
                g.wait_ge(dbg5.h, dbg5.n)
            return nc, dbg

        pt_stack.close()
        PS6 = en(nc.psum_tensor("ps6", [128, 512], F32))
        PS7 = en(nc.psum_tensor("ps7", [128, 512], F32))
        PSX = PS + [PS6, PS7]

        HACT6 = [j * 1024 for j in range(4)]
        HRES = 65536
        NRES = 28

        def hact_slot(c8, t2, n):
            if t2 < NRES:
                return HRES + t2 * 1024
            nr = c8 * (HT - NRES) + (t2 - NRES)
            return HACT6[nr % 4]
        W2T6 = [4096 + j * 512 for j in range(20)]
        SG26 = [14336 + j * 512 for j in range(8)]
        XR6 = [16, 528]
        G6 = 1040
        OUTST = [1552, 2064]
        W2RW6 = [0, 512]
        TH6 = [1024, 1536]
        TL6 = [2048, 2560]

        sH6 = [S() for _ in range(4)]
        sHR = S()
        sL26 = [S() for _ in range(4)]
        sDQ6 = S()
        sRD6 = S()
        sM6 = S()
        sSIL6 = S()
        sCB = S()
        sXR = [S(), S()]
        sOW = [S(), S()]

        NP6 = 8 * 43

        @block.sync
        def _(s):
            def emit_loads6(s, c8):
                for t2 in range(HT):
                    n = c8 * HT + t2
                    if t2 < NRES and c8 >= 1:
                        continue
                    if t2 < NRES:
                        s.dma_start(
                            out=b16(HRES + t2 * 1024, 1024),
                            in_=bass.AP(hact_h.tensor, t2 * 128 * QTOK,
                                        [[QTOK, 128], [1, QTOK]])) \
                            .then_inc(sHR.inc(16).h, 16)
                        continue
                    nr = c8 * (HT - NRES) + (t2 - NRES)
                    if nr >= 4:
                        prev_t2 = NRES + (nr - 4) % (HT - NRES)
                        prev_c8 = (nr - 4) // (HT - NRES)
                        s.wait_ge(sM6.h, prev_c8 * HT + prev_t2 + 1)
                    s.dma_start(
                        out=b16(HACT6[nr % 4], 1024),
                        in_=bass.AP(hact_h.tensor, t2 * 128 * QTOK,
                                    [[QTOK, 128], [1, QTOK]])) \
                        .then_inc(sH6[nr % 4].inc(16).h, 16)

            for c8 in range(8):
                emit_loads6(s, c8)

        @block.gpsimd
        def _(g):
            def emit_xr(g, c8x, m):
                idx = c8x * 8 + m
                if idx >= 2:
                    g.wait_ge(sCB.h, idx - 1)
                g.dma_start(
                    out=b32(XR6[idx % 2], 512),
                    in_=bass.AP(ctx.tensor,
                                (1024 + m * 128) * D + c8x * 512,
                                [[D, 128], [1, 512]])) \
                    .then_inc(sXR[idx % 2].inc(16).h, 16)

            for p in range(NP6):
                c8, u2 = p // 43, p % 43
                if c8 >= 1 and u2 < 8:
                    emit_xr(g, c8 - 1, u2)
                sl = p % 4
                sl2 = sl * 2
                if p >= 2:
                    g.wait_ge(sRD6.h, p - 1)
                if p >= 4:
                    g.wait_ge(sDQ6.h, p - 3)
                g.dma_start(
                    out=bint(W2RW6[p % 2], 512),
                    in_=bass.AP(pk2.tensor, 2 * u2 * 64 * D + c8 * 512,
                                [[64 * D, 2], [D, 64], [1, 512]])) \
                    .then_inc(sL26[sl].inc(16).h, 16)
                g.dma_start(
                    out=b16(SG26[sl2], 512),
                    in_=bass.AP(s2b.tensor, 4 * u2 * D + c8 * 512,
                                [[2 * D, 2], [0, 64], [1, 512]])) \
                    .then_inc(sL26[sl].inc(16).h, 16)
                g.dma_start(
                    out=b16(SG26[sl2 + 1], 512),
                    in_=bass.AP(s2b.tensor, (4 * u2 + 1) * D + c8 * 512,
                                [[2 * D, 2], [0, 64], [1, 512]])) \
                    .then_inc(sL26[sl].inc(16).h, 16)
            for m in range(8):
                emit_xr(g, 7, m)

        @block.vector
        def _(v):
            def emit_deq_pair(v, c8, u2):
                p = c8 * 43 + u2
                n = c8 * HT + 2 * u2
                sl = p % 4
                sl2 = sl * 2
                tp = p % 2
                v.wait_ge(sL26[sl].h, 48 * (p // 4 + 1))
                if n + 1 >= 20:
                    v.wait_ge(sM6.h, n + 1 - 20 + 1)
                v.tensor_scalar(out=bint(TH6[tp], 512),
                                in0=bint(W2RW6[p % 2], 512),
                                scalar1=4, scalar2=None,
                                op0=ALU.arith_shift_right)
                v.tensor_scalar(out=bint(TL6[tp], 512),
                                in0=bint(W2RW6[p % 2], 512),
                                scalar1=15, scalar2=8,
                                op0=ALU.bitwise_and, op1=ALU.bitwise_xor) \
                    .then_inc(sRD6.inc().h, 1)
                v.drain()
                v.tensor_tensor(out=b16(W2T6[n % 20], 512),
                                in0=bint(TH6[tp], 512),
                                in1=b16(SG26[sl2], 512), op=ALU.mult)
                v.scalar_tensor_tensor(
                    out=b16(W2T6[(n + 1) % 20], 512),
                    in0=bint(TL6[tp], 512), scalar=-8,
                    in1=b16(SG26[sl2 + 1], 512),
                    op0=ALU.add, op1=ALU.mult).then_inc(sDQ6.inc().h, 1)

            def emit_comb_m(v, c8, m):
                idx = c8 * 8 + m
                v.wait_ge(sSIL6.h, idx + 1)
                v.tensor_tensor(out=b32(G6, 512),
                                in0=b16(SS + m * 4096 + c8 * 512, 512),
                                in1=b32(G6, 512), op=ALU.mult)
                v.drain()
                v.wait_ge(sXR[idx % 2].h, 16 * (idx // 2 + 1))
                if idx >= 2:
                    v.wait_ge(sOW[idx % 2].h, 16 * (idx // 2))
                v.tensor_tensor(out=b32(OUTST[idx % 2], 512),
                                in0=b32(G6, 512),
                                in1=b32(XR6[idx % 2], 512), op=ALU.add) \
                    .then_inc(sCB.inc().h, 1)

            for c8 in range(8):
                for u2 in range(43):
                    emit_deq_pair(v, c8, u2)
                    if c8 >= 1 and 1 <= u2 <= 8:
                        emit_comb_m(v, c8 - 1, u2 - 1)
            for m in range(8):
                emit_comb_m(v, 7, m)

        @block.tensor
        def _(t):
            for c8 in range(8):
                for t2 in range(HT):
                    n = c8 * HT + t2
                    p = c8 * 43 + t2 // 2
                    if t2 < NRES:
                        if c8 == 0 and t2 == 0:
                            t.wait_ge(sHR.h, 16 * NRES)
                    else:
                        nr = c8 * (HT - NRES) + (t2 - NRES)
                        t.wait_ge(sH6[n % 4].h, 16 * (nr // 4 + 1))
                    t.wait_ge(sDQ6.h, p + 1)
                    hs = hact_slot(c8, t2, n)
                    for m in range(8):
                        if c8 >= 1 and t2 == 0:
                            t.wait_ge(sSIL6.h, 8 * (c8 - 1) + m + 1)
                        ins = t.matmul(PSX[m][:, 0:512],
                                       b16(hs + m * 128, 128),
                                       b16(W2T6[n % 20], 512),
                                       start=(t2 == 0), stop=(t2 == HT - 1))
                        if m == 7:
                            ins.then_inc(sM6.inc().h, 1)

        @block.scalar
        def _(sc):
            def emit_out6(sc, idx):
                c8o, mo = idx // 8, idx % 8
                sc.wait_ge(sCB.h, idx + 1)
                sc.dma_start(
                    out=bass.AP(out.tensor, mo * 128 * D + c8o * 512,
                                [[D, 128], [1, 512]]),
                    in_=b32(OUTST[idx % 2], 512)) \
                    .then_inc(sOW[idx % 2].inc(16).h, 16)

            for c8 in range(8):
                for m in range(8):
                    idx = c8 * 8 + m
                    sc.wait_ge(sM6.h, HT * (c8 + 1))
                    if idx >= 1:
                        sc.wait_ge(sCB.h, idx)
                    sc.activation(out=b32(G6, 512), in_=PSX[m][:, 0:512],
                                  func=AF.Silu).then_inc(sSIL6.inc().h, 1)
                    if idx >= 2:
                        emit_out6(sc, idx - 2)
            emit_out6(sc, 62)
            emit_out6(sc, 63)
            sc.wait_ge(sOW[0].h, 16 * 32)
            sc.wait_ge(sOW[1].h, 16 * 32)

        barrier()
        return nc, dbg
    return nc, dbg



def prep_inputs(L0_attn_x, L0_attn_wq, L0_attn_wk, L0_attn_wv, L0_attn_wo,
                L0_ffn_w1, L0_ffn_s1, L0_ffn_w2, L0_ffn_s2, L0_ffn_w3,
                L0_ffn_s3):
    pd = sig_swap(D)
    ph = sig_swap(HID)

    def w_blk(w):
        wr = np.asarray(w)[pd]
        n_ob = wr.shape[1] // 128
        return np.ascontiguousarray(
            wr.reshape(32, 128, n_ob, 128).transpose(2, 1, 0, 3)
        ).reshape(n_ob * 128, 4096)

    wq_r = w_blk(L0_attn_wq)
    wk_r = w_blk(L0_attn_wk)
    wv_r = w_blk(L0_attn_wv)
    wo_n = np.ascontiguousarray(np.asarray(L0_attn_wo))

    def prep_w13(w):
        w = np.asarray(w).reshape(HID, 32, 64)[ph]
        return np.ascontiguousarray(w.transpose(1, 2, 0))

    def prep_s13(s):
        return np.ascontiguousarray(np.asarray(s).reshape(HID, 64)[ph].T)

    pk1 = prep_w13(L0_ffn_w1)
    pk3 = prep_w13(L0_ffn_w3)
    s1r = prep_s13(L0_ffn_s1)
    s3r = prep_s13(L0_ffn_s3)
    pk2 = np.ascontiguousarray(
        np.asarray(L0_ffn_w2).reshape(D, 86, 64).transpose(1, 2, 0))
    s2r = np.ascontiguousarray(np.asarray(L0_ffn_s2).reshape(D, 172).T)

    x = np.asarray(L0_attn_x)

    inv = 10000.0 ** (-np.arange(64, dtype=np.float64) / 64.0)
    kc = np.arange(SEQ)
    ql = np.arange(QTOK)

    per_half = []
    for half in (0, 1):
        if half == 1:
            qpos = ql + 1024
            kpos = kc.astype(np.float64)
            allowed = (kc[:, None] <= qpos[None, :]) & \
                      (kc[:, None] > qpos[None, :] - WINDOW)
        else:
            qpos = ql
            kpos = np.concatenate([np.arange(1024), np.arange(1024)]).astype(np.float64)
            kp = kc - 1024
            allowed = (kc[:, None] >= 1024) & (kp[:, None] <= ql[None, :])
        def tabs(pos):
            ang = pos[None, :] * inv[:, None]
            c, si = np.cos(ang), np.sin(ang)
            t1 = np.concatenate([c, c], axis=0)
            t2 = np.concatenate([-si, si], axis=0)
            return (t1.astype(ml_dtypes.bfloat16),
                    t2.astype(ml_dtypes.bfloat16))

        tq1, tq2 = tabs(qpos.astype(np.float64))
        tk1, tk2 = tabs(kpos)
        per_half.append(dict(
            maskT=np.where(allowed, 0.0, NEG).astype(ml_dtypes.bfloat16),
            tabq1=tq1, tabq2=tq2, tabk1=tk1, tabk2=tk2,
        ))

    ident = np.eye(128, dtype=ml_dtypes.bfloat16)

    in_maps = []
    for core in range(8):
        b, half = core // 2, core % 2
        ctxc = x[b] if half == 1 else \
            np.concatenate([x[b, 0:1024], x[b, 0:1024]], axis=0)
        in_maps.append(dict(
            ctx=np.ascontiguousarray(ctxc), wq=wq_r, wk=wk_r, wv=wv_r,
            wo=wo_n, pk1=pk1, pk3=pk3, pk2=pk2, s1r=s1r, s3r=s3r, s2r=s2r,
            identw=ident, **per_half[half]))
    return in_maps


def kernel(**inputs):
    in_maps = prep_inputs(**inputs)
    nc, _ = build_program()
    res = run_bass_kernel_spmd(nc, in_maps, core_ids=list(range(8)))
    o = np.zeros((4, SEQ, D), dtype=np.float32)
    for core in range(8):
        b, half = core // 2, core % 2
        o[b, half * 1024:(half + 1) * 1024] = res.results[core]["out"]
    return o
